# version 11
# baseline (speedup 1.0000x reference)
import contextlib
import numpy as np

import concourse.bass as bass
import concourse.mybir as mybir
from concourse.bass_utils import run_bass_kernel_spmd
from concourse.bass import IndirectOffsetOnAxis

F32 = mybir.dt.float32
BF16 = mybir.dt.bfloat16
AF = mybir.ActivationFunctionType
ALU = mybir.AluOpType
AX = mybir.AxisListType


class Cfg:
    def __init__(self, D=2048, T=1024, NH=32, E=16, I=1408, Is=5632, TOPK=4, SW=None):
        self.D, self.T, self.NH, self.E, self.I, self.Is = D, T, NH, E, I, Is
        self.QKB = 8
        self.TOPK = TOPK
        self.DC = D // 128
        self.TT = T // 128
        self.TX = T + 128
        self.NKT = self.TX // 128
        self.G = NH // 4
        self.QR = NH * self.QKB
        self.JC = max(1, self.QR // 128)
        self.CV = max(1, self.QR // 128)
        self.AVW = max(self.QR, 128)
        self.IC = I // 128
        self.IsC = Is // 128
        self.NHG = self.IsC // self.IC
        self.SW = SW or min(512, D)
        self.NSL = D // self.SW
        self.TW = min(512, T)
        self.NTS = T // self.TW
        self.ER = E + 1
        mean = T * TOPK // E
        self.C = 128 * ((3 * mean // 2 + 127) // 128)
        self.CT = self.C // 128
        self.C2 = 2 * self.C
        self.CT2 = self.C2 // 128


REAL = Cfg()


def arr_lhsT(W, fout, fin):
    jF, cK = fout // 128, fin // 128
    return np.ascontiguousarray(
        W.reshape(jF, 128, cK, 128).transpose(0, 3, 2, 1).reshape(jF, 128, fin))


def arr_rhs(W, fout, fin):
    cK = fin // 128
    return np.ascontiguousarray(W.reshape(fout, cK, 128).transpose(1, 2, 0))


def prepare_inputs(cfg, inp):
    g = lambda k: np.ascontiguousarray(np.asarray(inp[k], dtype=np.float32))
    D, T, E, I, Is, QR = cfg.D, cfg.T, cfg.E, cfg.I, cfg.Is, cfg.QR

    x = g("L0_attn_hidden_states")
    B, S, _ = x.shape
    x2 = x.reshape(B * S, D)

    Wq, Wk, Wv = g("L0_attn_Wq"), g("L0_attn_Wk"), g("L0_attn_Wv")
    Wo, Wg = g("L0_attn_Wo"), g("L0_attn_Wg")
    v0, v1 = g("L0_attn_v_emb0"), g("L0_attn_v_emb1")
    gw, sgw = g("L0_ffn_gate_w"), g("L0_ffn_sgate_w")
    w13, w2 = g("L0_ffn_w13"), g("L0_ffn_w2")
    sgu, sdw = g("L0_ffn_shared_gate_up_w"), g("L0_ffn_shared_down_w")

    QP = 128 * cfg.JC
    if QP != QR:
        Wq = np.concatenate([Wq, np.zeros((QP - QR, D), np.float32)], 0)
        Wk = np.concatenate([Wk, np.zeros((QP - QR, D), np.float32)], 0)
    VP = 128 * cfg.CV
    Wo_p = Wo
    if VP != QR:
        Wo_p = np.concatenate([Wo, np.zeros((D, VP - QR), np.float32)], 1)
        v0p = np.concatenate([v0, np.zeros(VP - QR, np.float32)])
    else:
        v0p = v0

    sh = {}
    sh["wq_a"] = arr_lhsT(Wq, QP, D)
    sh["wk_a"] = arr_lhsT(Wk, QP, D)
    sh["wv_a"] = arr_rhs(Wv, QR, D)
    sh["wr_a"] = arr_rhs(np.concatenate([gw, sgw], 0), E + 1, D)
    sh["wo_a"] = arr_rhs(Wo_p, D, VP)
    sh["wg_a"] = np.ascontiguousarray(
        Wg.reshape(cfg.NSL, cfg.SW, cfg.DC, 128).transpose(0, 2, 3, 1))
    gu_t = arr_lhsT(sgu, 2 * Is, D)
    IsC = cfg.IsC
    sh["gu_a"] = np.ascontiguousarray(np.stack(
        [np.concatenate([gu_t[j], gu_t[IsC + j]], axis=1) for j in range(IsC)]))
    sh["dn_a"] = np.ascontiguousarray(
        sdw.T.reshape(cfg.NHG, cfg.IC, 128, cfg.NSL, cfg.SW).transpose(0, 3, 1, 2, 4))
    w13_l = [arr_lhsT(w13[e], 2 * I, D) for e in range(E)]
    IC = cfg.IC
    sh["w13_a"] = np.ascontiguousarray(np.stack(
        [np.stack([np.concatenate([w13_l[e][j], w13_l[e][IC + j]], axis=1)
                   for j in range(IC)]) for e in range(E)]))
    sh["w2_a"] = np.ascontiguousarray(np.stack([
        w2[e].T.reshape(cfg.IC, 128, cfg.NSL, cfg.SW).transpose(2, 0, 1, 3)
        for e in range(E)]))
    sh["vemb_row"] = np.ascontiguousarray(np.concatenate([v0, v1]).reshape(1, 2 * QR))
    sh["vemb0c"] = np.ascontiguousarray(v0p.reshape(cfg.CV, 128).T)
    sh["ident"] = np.eye(128, dtype=np.float32)
    sh["tri"] = np.tril(np.ones((128, 128), np.float32), -1)

    n_i = np.arange(256)[None, :].astype(np.float64)
    p_i = np.arange(128)[:, None].astype(np.float64)
    dd = n_i - p_i
    dec = np.where(dd >= 0, np.exp(np.where(dd >= 0, dd * np.log(0.45), 0.0)), 0.0)
    dec = dec.astype(np.float32)
    sh["decay"] = dec

    TC = 2 * T
    in_maps = []
    for k in range(B * S // TC):
        g0 = k * TC
        xc = np.empty((TC + 128, D), np.float32)
        if g0 % S == 0:
            xc[:128] = 0.0
            dec0 = np.zeros_like(dec)
        else:
            xc[:128] = x2[g0 - 128:g0]
            dec0 = dec
        xc[128:] = x2[g0:g0 + TC]
        m = {"x": xc, "decay0": np.ascontiguousarray(dec0)}
        m.update(sh)
        in_maps.append(m)
    return in_maps, (B, S)


class _Op:
    __slots__ = ("eng", "kind", "args", "deps", "inc", "cnt", "semkey", "dma")

    def __init__(self, eng, kind, args, semkey, dma):
        self.eng, self.kind, self.args = eng, kind, args
        self.deps, self.inc, self.cnt = [], False, None
        self.semkey, self.dma = semkey, dma


class Plan:
    def __init__(self):
        self.phases = []
        self.cur = None
        self.res = {}
        self.all_sems = set()

    def phase(self, name):
        self.cur = []
        self.phases.append((name, self.cur))

    def op(self, eng, kind, reads=(), writes=(), waw=True, dmasem=None, **args):
        dma = kind == "dma"
        semkey = dmasem if dma else ("E" + eng)
        self.all_sems.add(semkey)
        o = _Op(eng, kind, args, semkey, dma)
        deps = []
        for r in reads:
            e = self.res.get(r)
            if e:
                deps.extend(e["w"].values())
        for w in writes:
            e = self.res.get(w)
            if e:
                deps.extend(e["r"].values())
                if waw:
                    deps.extend(e["w"].values())
        seen = set()
        for d in deps:
            if d is o or id(d) in seen:
                continue
            seen.add(id(d))
            if o.eng == "t" and d.eng == "t" and not d.dma:
                continue
            o.deps.append(d)
        rkey = semkey if dma else o.eng
        for r in reads:
            self.res.setdefault(r, {"w": {}, "r": {}})["r"][rkey] = o
        for w in writes:
            e = self.res.setdefault(w, {"w": {}, "r": {}})
            e["w"][rkey] = o
            e["r"] = {}
        self.cur.append(o)
        return o

    def finalize(self):
        for _, ops in self.phases:
            for o in ops:
                for d in o.deps:
                    if not d.dma:
                        d.inc = True
        cnt = {}
        for _, ops in self.phases:
            for o in ops:
                if o.dma:
                    cnt[o.semkey] = cnt.get(o.semkey, 0) + 16
                    o.cnt = cnt[o.semkey]
                elif o.inc:
                    cnt[o.semkey] = cnt.get(o.semkey, 0) + 1
                    o.cnt = cnt[o.semkey]
        self.final_counts = cnt


ENGSEC = {"t": "tensor", "v": "vector", "a": "scalar", "g": "gpsimd", "s": "sync"}


def _emit_op(eng, o, bregs=None):
    k, a = o.kind, o.args
    if k == "dma":
        if a.get("kindx"):
            dst, out_off = a["dst"]
            srcp, in_off = a["src"]
            bv = a["bc"]
            if bregs is not None:
                if bv not in bregs:
                    bregs[bv] = eng.to_reg(bv)
                bv = bregs[bv]
            return eng.indirect_dma_start(
                dst, out_off, srcp, in_off,
                bounds_check=bv, oob_is_err=False,
                compute_op=a["cop"] if a["cop"] is not None else ALU.bypass)
        return eng.dma_start(a["dst"], a["src"])
    if k == "tts":
        return eng.tensor_tensor_scan(a["out"], a["in0"], a["in1"], 0.0,
                                      ALU.add, ALU.bypass)
    if k == "mm":
        return eng.matmul(a["out"], a["lhsT"], a["rhs"], start=a["start"],
                          stop=a["stop"], tile_position=a.get("tp"))
    if k == "tr":
        return eng.transpose(a["out"], a["in_"], a["ident"])
    if k == "act":
        kw = {}
        for f in ("bias", "scale"):
            if f in a:
                kw[f] = a[f]
        if "accum" in a:
            kw["accum_out"] = a["accum"]
        return eng.activation(a["out"], a["in_"], a["fn"], **kw)
    if k == "preload":
        return eng.preload_act_table(a["fn"])
    if k == "tt":
        return eng.tensor_tensor(a["out"], a["in0"], a["in1"], a["op"])
    if k == "ts":
        return eng.tensor_scalar(a["out"], a["in0"], a["s1"], a.get("s2"), a["op0"],
                                 a.get("op1", ALU.bypass))
    if k == "stt":
        return eng.scalar_tensor_tensor(a["out"], a["in0"], a["s"], a["in1"],
                                        a["op0"], a["op1"])
    if k == "copy":
        return eng.tensor_copy(a["out"], a["in_"])
    if k == "red":
        return eng.tensor_reduce(a["out"], a["in_"], AX.X, a["op"],
                                 negate=a.get("negate", False))
    if k == "recip":
        return eng.reciprocal(a["out"], a["in_"])
    if k == "memset":
        return eng.memset(a["out"], a["val"])
    raise ValueError(k)


def emit_phase(nc, ops, sems, watermark):
    engs = [e for e in ("g", "s", "t", "v", "a") if any(o.eng == e for o in ops)]
    with nc.Block() as blk:
        for e in engs:
            def body(eng, e=e):
                wm = watermark.setdefault(e, {})
                bregs = {}
                for o in ops:
                    if o.eng != e:
                        continue
                    waits = {}
                    for d in o.deps:
                        if d.cnt is None:
                            continue
                        if d.cnt > waits.get(d.semkey, 0):
                            waits[d.semkey] = d.cnt
                    for k in sorted(waits):
                        v = waits[k]
                        if v > wm.get(k, 0):
                            eng.wait_ge(sems[k], v)
                            wm[k] = v
                    ins = _emit_op(eng, o, bregs)
                    if o.dma:
                        ins.then_inc(sems[o.semkey], 16)
                    elif o.inc:
                        ins.then_inc(sems[o.semkey], 1)
            getattr(blk, ENGSEC[e])(body)


def build_nc(cfg, dump=False, prezero=False):
    nc = bass.Bass("TRN2", target_bir_lowering=False)
    D, T, TX, DC, TT, NKT = cfg.D, cfg.T, cfg.TX, cfg.DC, cfg.TT, cfg.NKT
    G, JC, CV, QR, E, ER = cfg.G, cfg.JC, cfg.CV, cfg.QR, cfg.E, cfg.ER
    AVW = cfg.AVW
    IC, IsC, NHG = cfg.IC, cfg.IsC, cfg.NHG
    SW, NSL, TW, NTS = cfg.SW, cfg.NSL, cfg.TW, cfg.NTS
    QP = 128 * JC
    VP = 128 * CV

    xin = nc.dram_tensor("x", [2 * T + 128, D], F32, kind="ExternalInput").ap()
    dec_in = nc.dram_tensor("decay", [128, 256], F32, kind="ExternalInput").ap()
    dec0_in = nc.dram_tensor("decay0", [128, 256], F32, kind="ExternalInput").ap()
    ident_in = nc.dram_tensor("ident", [128, 128], F32, kind="ExternalInput").ap()
    wq_in = nc.dram_tensor("wq_a", [JC, 128, D], F32, kind="ExternalInput").ap()
    wk_in = nc.dram_tensor("wk_a", [JC, 128, D], F32, kind="ExternalInput").ap()
    wv_in = nc.dram_tensor("wv_a", [DC, 128, QR], F32, kind="ExternalInput").ap()
    wr_in = nc.dram_tensor("wr_a", [DC, 128, ER], F32, kind="ExternalInput").ap()
    wo_in = nc.dram_tensor("wo_a", [CV, 128, D], F32, kind="ExternalInput").ap()
    wg_in = nc.dram_tensor("wg_a", [NSL, DC, 128, SW], F32, kind="ExternalInput").ap()
    gu_in = nc.dram_tensor("gu_a", [IsC, 128, 2 * D], F32, kind="ExternalInput").ap()
    dn_in = nc.dram_tensor("dn_a", [NHG, NSL, IC, 128, SW], F32,
                           kind="ExternalInput").ap()
    w13_in = nc.dram_tensor("w13_a", [E, IC, 128, 2 * D], F32, kind="ExternalInput").ap()
    w2_in = nc.dram_tensor("w2_a", [E, NSL, IC, 128, SW], F32, kind="ExternalInput").ap()
    vrow_in = nc.dram_tensor("vemb_row", [1, 2 * QR], F32, kind="ExternalInput").ap()
    v0c_in = nc.dram_tensor("vemb0c", [128, CV], F32, kind="ExternalInput").ap()
    out = nc.dram_tensor("out", [2 * T, D], F32, kind="ExternalOutput").ap()
    tri_in = nc.dram_tensor("tri", [128, 128], F32, kind="ExternalInput").ap()
    C, CT = cfg.C, cfg.CT
    xn_dram = nc.dram_tensor("xn_dram", [cfg.TX, D], BF16, kind="Internal").ap()
    C2, CT2 = cfg.C2, cfg.CT2
    xg_glob = nc.dram_tensor("xg_glob", [E * C2, D], BF16, kind="Internal").ap()
    og_glob = nc.dram_tensor("og_glob", [E * C2, D], F32, kind="Internal").ap()
    gc_glob = nc.dram_tensor("gc_glob", [E * C2, E], F32, kind="Internal").ap()
    acc_dram = nc.dram_tensor("acc_dram", [T, D], F32, kind="Internal").ap()

    P = Plan()
    es = contextlib.ExitStack()
    with es:
        sb = lambda n, sh_, dt=BF16: es.enter_context(nc.sbuf_tensor(n, sh_, dt))
        XN = max(DC * TX, DC * cfg.C2)
        xn_fm = sb("xn_fm", [128, XN])
        acc = sb("acc", [128, TT * D], F32)
        c0_bc = sb("c0_bc", [128, D], F32)
        vembd_bc = sb("vembd_bc", [128, QR], F32)
        decay_b = sb("decay_b", [128, 256])
        decay0_b = sb("decay0_b", [128, 256])
        ident_f = sb("ident_f", [128, 128], F32)
        ident_b = sb("ident_b", [128, 128])
        ones_row = sb("ones_row", [1, 128], F32)
        stats = sb("stats", [128, 8], F32)
        coef_sb = sb("coef_sb", [128, TT * E], F32)
        sg_sb = sb("sg_sb", [128, TT], F32)
        logits = sb("logits", [128, TT * ER], F32)
        rsum4 = sb("rsum4", [128, 4], F32)
        av_sb = sb("av_sb", [128, 128], F32)
        avn_t1 = sb("avn_t1", [128, 32], F32)
        rr = sb("rr", [128, 8], F32)
        cst = sb("cst", [128, 2], F32)
        tri_b = sb("tri_b", [128, 128])
        mask_sb = sb("mask_sb", [128, 0], F32) if False else None
        pexp = sb("pexp", [128, E], F32)
        pn = [sb(f"pn{i}", [128, E], F32) for i in range(2)]
        msk = sb("msk", [128, E], F32)
        mask_sb = sb("mask_s", [128, TT * E], F32)
        posI = sb("posI", [128, 2 * TT * 4], mybir.dt.int32)
        scn = sb("scn", [128, TT], F32)
        excl = sb("excl", [128, TT], F32)
        posf = sb("posf", [128, TT], F32)
        rt_b = sb("rt_b", [128, 1])
        selr = sb("selr", [128, TT * E], F32)
        slotacc = sb("slotacc", [128, TT * 4], F32)
        selsum = sb("selsum", [128, TT * 4], F32)
        gcoef = [sb(f"gcoef{j}", [128, CT2 * E], F32) for j in range(2)]
        TMPW = max(SW, TW, min(512, cfg.C2))
        tmpA = sb("tmpA", [128, TMPW], F32)
        tmpB = sb("tmpB", [128, TMPW], F32)
        x_t = sb("x_t", [128, D], F32)
        xn_t = sb("xn_t", [128, D])
        vaug = sb("vaug", [128, 2 * 36])
        v0c_sb = sb("v0c_sb", [128, CV])

        a_q, a_k = 0, JC * T
        a_v = a_k + JC * TX
        a_an = a_v + NKT * QR
        a_af = a_an + TT * AVW
        a_end = a_af + CV * T
        f_xgt = 0
        f_xgf = D
        f_h = f_xgf + DC * C
        AR1 = max(a_end, IC * T, f_h + IC * C)
        arena = sb("arena", [128, AR1])
        ar_cuts = sorted({0, a_k, a_v, a_an, a_af, a_end, IC * T, AR1,
                          f_xgf, f_h, f_h + IC * C})

        w_wq, w_wk = 0, JC * D
        w_wv = w_wk + JC * D
        w_wr = w_wv + DC * QR
        w_endB = w_wr + DC * ER
        w_wgA, w_wgB = 0, DC * SW
        w_wo = 2 * DC * SW
        w_endD = w_wo + CV * D
        w_ggA, w_ggB = 0, D
        w_guA, w_guB = 2 * D, 3 * D
        w_dnA = 4 * D
        w_dnB = w_dnA + IC * SW
        w_endF = w_dnB + IC * SW
        a_QG = 0
        a_KG = a_QG + T
        a_PB = a_KG + TX
        w_endC = a_PB + 2 * 1024
        WB = max(w_endB, w_endD, w_endF, w_endC)
        wbig = sb("wbig", [128, WB])
        wb_cuts = sorted({0, w_wk, w_wv, w_wr, w_endB, w_wgB, w_wo, w_endD,
                          w_ggB, w_guA, w_guB, w_dnA, w_dnB, w_endF, WB,
                          a_KG, a_PB, a_PB + 1024, w_endC})

        def _regs(cuts, pfx, c0, c1):
            names = []
            for i in range(len(cuts) - 1):
                if cuts[i] < c1 and cuts[i + 1] > c0:
                    names.append(f"{pfx}{i}")
            return names

        ar = lambda c0, c1: _regs(ar_cuts, "AR", c0, c1)
        wb = lambda c0, c1: _regs(wb_cuts, "WB", c0, c1)

        ps = lambda n, sh_, dt=F32: es.enter_context(nc.psum_tensor(n, sh_, dt))
        pT = [ps(f"pT{i}", [128, 1024], BF16) for i in range(2)]
        pS4 = ps("pS4", [128, 2048])
        _px = [ps(f"pX{i}", [128, 512]) for i in range(2)]
        pF = [pS4[:, 512 * i:512 * (i + 1)] for i in range(4)] + _px

        mmop = lambda o, l, r, s0, s1, tp=None, rd=(), wr=(): P.op(
            "t", "mm", reads=rd, writes=wr, out=o, lhsT=l, rhs=r,
            start=s0, stop=s1, tp=tp)
        xn_off = lambda c, tok: c * TX + tok
        kaop = lambda rd: mmop(pF[0][:, 400:401], ones_row[0:1, 0:128],
                               ones_row[0:1, 0:1], True, True,
                               rd=list(rd) + ["ones_row"], wr=["pF0"])
        XF = ["xfA", "xfB"]

        P.phase("consts")
        P.op("g", "memset", writes=["vaug"], out=vaug[:, :], val=1.0)
        P.op("g", "memset", writes=["cst"], out=cst[:, 0:1], val=1e-5)
        P.op("g", "memset", writes=["cst"], waw=False, out=cst[:, 1:2], val=0.125)
        P.op("g", "memset", writes=ar(a_an, a_af), out=arena[:, a_an:a_af], val=0.0)
        P.op("v", "memset", writes=["pF4"], out=pF[4][:, :], val=0.0)
        P.op("v", "memset", writes=["pF5"], out=pF[5][:, :], val=0.0)
        P.op("s", "dma", writes=["ident_f"], dmasem="dc0", dst=ident_f[:, :], src=ident_in)
        P.op("s", "dma", writes=["tmpA"], dmasem="dc1", dst=tmpA[0:1, 0:2 * QR],
             src=vrow_in)
        P.op("g", "dma", writes=["decay_b"], dmasem="dd0", dst=decay_b[:, :], src=dec_in)
        P.op("g", "dma", writes=["decay0_b"], dmasem="dd1", dst=decay0_b[:, :],
             src=dec0_in)
        P.op("g", "dma", writes=["v0c_sb"], dmasem="dd2", dst=v0c_sb[:, :], src=v0c_in)
        P.op("v", "copy", reads=["ident_f"], writes=["ident_b"], out=ident_b[:, :],
             in_=ident_f[:, :])
        P.op("v", "memset", writes=["ones_row"], out=ones_row[:, :], val=1.0)
        P.op("v", "tt", reads=["tmpA"], writes=["tmpB"], out=tmpB[0:1, 0:QR],
             in0=tmpA[0:1, QR:2 * QR], in1=tmpA[0:1, 0:QR], op=ALU.subtract)
        mmop(pF[0][:, 0:QR], ones_row[0:1, 0:128], tmpB[0:1, 0:QR], True, True,
             rd=["ones_row", "tmpB"], wr=["pF0"])
        P.op("v", "copy", reads=["pF0"], writes=["vembd_bc"], out=vembd_bc[:, :],
             in_=pF[0][:, 0:QR])
        P.op("g", "dma", writes=["tri_b"], dmasem="dtr", dst=tri_b[:, :], src=tri_in)
        P.op("g", "memset", writes=["x_t"], out=x_t[:, :], val=0.0)

        if prezero:
            for i in range(E * CT2):
                P.op("g", "dma", reads=["x_t"], writes=["xgg"], waw=False, dmasem="pz",
                     dst=xg_glob[i * 128:(i + 1) * 128, :], src=x_t[:, 0:D])
                P.op("g", "dma", reads=["x_t"], writes=["gcg"], waw=False, dmasem="pz",
                     dst=gc_glob[i * 128:(i + 1) * 128, :], src=x_t[:, 0:E])

        for h in range(2):
            half0 = h * T

            P.phase(f"ln{h}")
            P.op("a", "preload", fn=AF.Sqrt)
            xalt = arena[:, 0:2 * D].bitcast(F32)
            for kt in range(NKT):
                r0 = half0 + kt * 128
                if kt % 2 == 0:
                    xt_b, xt_rl = x_t, ["x_t"]
                else:
                    xt_b, xt_rl = xalt, ar(0, 2 * D)
                xt_r = xt_rl
                P.op("s", "dma", writes=xt_r, waw=False, dmasem=f"dx{kt % 2}",
                     dst=xt_b[:, 0:D], src=xin[r0:r0 + 128, :])
                kaop(xt_r)
                ST = stats
                P.op("v", "red", reads=xt_r, writes=["s1"], out=ST[:, 0:1],
                     in_=xt_b[:, 0:D], op=ALU.add)
                kaop(["s1"])
                P.op("a", "act", reads=xt_r, writes=["xn_t", "ssq"],
                     out=xn_t[:, :], in_=xt_b[:, 0:D], fn=AF.Square, accum=ST[:, 1:2])
                P.op("v", "ts", reads=["s1"], writes=["mv"], out=ST[:, 2:3],
                     in0=ST[:, 0:1], s1=1.0 / D, op0=ALU.mult)
                P.op("v", "tt", reads=["mv"], writes=["msq"], out=ST[:, 3:4],
                     in0=ST[:, 2:3], in1=ST[:, 2:3], op=ALU.mult)
                P.op("v", "stt", reads=["ssq", "msq"], writes=["var"], out=ST[:, 4:5],
                     in0=ST[:, 1:2], s=1.0 / D, in1=ST[:, 3:4], op0=ALU.mult,
                     op1=ALU.subtract)
                kaop(["var"])
                P.op("a", "act", reads=["var", "cst"], writes=["std"], out=ST[:, 5:6],
                     in_=ST[:, 4:5], fn=AF.Sqrt, bias=cst[:, 0:1])
                P.op("v", "recip", reads=["std"], writes=["rstd"], out=ST[:, 6:7],
                     in_=ST[:, 5:6])
                P.op("v", "ts", reads=xt_r + ["mv", "rstd"], writes=["xn_t"],
                     out=xn_t[:, :], in0=xt_b[:, 0:D], s1=ST[:, 2:3], s2=ST[:, 6:7],
                     op0=ALU.subtract, op1=ALU.mult)
                P.op("s", "dma", reads=["xn_t"], writes=["xn_dram"], dmasem="xnw",
                     dst=xn_dram[kt * 128:(kt + 1) * 128, :], src=xn_t[:, :])
                for c in range(DC):
                    pp = c % 2
                    P.op("t", "tr", reads=["xn_t", "ident_b"], writes=[f"pT{pp}"],
                         out=pT[pp][:, 0:128], in_=xn_t[:, c * 128:(c + 1) * 128],
                         ident=ident_b[:, :])
                    P.op("v", "copy", reads=[f"pT{pp}"], writes=XF, waw=False,
                         out=xn_fm[:, xn_off(c, kt * 128):xn_off(c, kt * 128) + 128],
                         in_=pT[pp][:, 0:128])

            P.op("a", "preload", fn=AF.Sigmoid)
            P.op("g", "dma", writes=wb(w_wq, w_wk), dmasem="lwq",
                 dst=wbig[:, w_wq:w_wq + JC * D].rearrange("p (j f) -> p j f", j=JC),
                 src=wq_in.rearrange("j p f -> p j f"))
            P.op("g", "dma", writes=wb(w_wk, w_wv), dmasem="lwk",
                 dst=wbig[:, w_wk:w_wk + JC * D].rearrange("p (j f) -> p j f", j=JC),
                 src=wk_in.rearrange("j p f -> p j f"))
            P.op("g", "dma", writes=wb(w_wv, w_wr), dmasem="lwv",
                 dst=wbig[:, w_wv:w_wv + DC * QR].rearrange("p (c f) -> p c f", c=DC),
                 src=wv_in.rearrange("c p f -> p c f"))
            P.op("g", "dma", writes=wb(w_wr, w_endB), dmasem="lwr",
                 dst=wbig[:, w_wr:w_wr + DC * ER].rearrange("p (c f) -> p c f", c=DC),
                 src=wr_in.rearrange("c p f -> p c f"))
            for which in ("q", "k"):
                base = w_wq if which == "q" else w_wk
                wres = wb(base, base + JC * D)
                if which == "q":
                    offs = [(128 + s * TW, TW) for s in range(NTS)]
                else:
                    offs = [(s * TW, TW) for s in range(NTS)] + [(T, 128)]
                step = 0
                for jc in range(JC):
                    for off, N in offs:
                        pb, pbr = pF[step % 2], f"pF{step % 2}"
                        step += 1
                        for c in range(DC):
                            mmop(pb[:, 0:N],
                                 wbig[:, base + jc * D + c * 128:
                                      base + jc * D + c * 128 + 128],
                                 xn_fm[:, xn_off(c, off):xn_off(c, off) + N],
                                 c == 0, c == DC - 1, rd=wres + XF, wr=[pbr])
                        if which == "q":
                            dcol = a_q + jc * T + (off - 128)
                            dres = ar(a_q, a_k)
                        else:
                            dcol = a_k + jc * TX + off
                            dres = ar(a_k, a_v)
                        P.op("a", "act", reads=[pbr], writes=dres, waw=False,
                             out=arena[:, dcol:dcol + N], in_=pb[:, 0:N], fn=AF.Tanh)
            for ckt in range(NKT):
                pv, pvr = pF[2 + ckt % 2], f"pF{2 + ckt % 2}"
                pr, prr = pF[4 + ckt % 2], f"pF{4 + ckt % 2}"
                for c in range(DC):
                    lh = xn_fm[:, xn_off(c, ckt * 128):xn_off(c, ckt * 128) + 128]
                    mmop(pv[:, 0:QR], lh, wbig[:, w_wv + c * QR:w_wv + (c + 1) * QR],
                         c == 0, c == DC - 1, rd=XF + wb(w_wv, w_wr), wr=[pvr])
                    if ckt >= 1:
                        mmop(pr[:, 0:ER], lh, wbig[:, w_wr + c * ER:w_wr + (c + 1) * ER],
                             c == 0, c == DC - 1, rd=XF + wb(w_wr, w_endB),
                             wr=[prr])
                P.op("a", "act", reads=[pvr], writes=ar(a_v, a_an), waw=False,
                     out=arena[:, a_v + ckt * QR: a_v + (ckt + 1) * QR],
                     in_=pv[:, 0:QR], fn=AF.Sigmoid)
                if ckt >= 1:
                    t_ = ckt - 1
                    P.op("v", "copy", reads=[prr], writes=["logits"], waw=False,
                         out=logits[:, t_ * ER:(t_ + 1) * ER], in_=pr[:, 0:ER])
                    P.op("a", "act", reads=[prr], writes=["sg_sb"], waw=False,
                         out=sg_sb[:, t_:t_ + 1], in_=pr[:, E:E + 1], fn=AF.Sigmoid)

            P.op("a", "preload", fn=AF.Exp)
            for t_ in range(TT):
                L = logits[:, t_ * ER: t_ * ER + E]
                P.op("v", "red", reads=["logits"], writes=["rr0"], out=rr[:, 0:1],
                     in_=L, op=ALU.max, negate=True)
                P.op("a", "act", reads=["rr0", "logits"], writes=["pexp"],
                     out=pexp[:, :], in_=L, fn=AF.Exp, bias=rr[:, 0:1])
                P.op("v", "red", reads=["pexp"], writes=["rr1"], out=rr[:, 1:2],
                     in_=pexp[:, :], op=ALU.add)
                P.op("v", "recip", reads=["rr1"], writes=["rr2"], out=rr[:, 2:3],
                     in_=rr[:, 1:2])
                P.op("v", "ts", reads=["pexp", "rr2"], writes=["pexp"], out=pexp[:, :],
                     in0=pexp[:, :], s1=rr[:, 2:3], op0=ALU.mult)
                P.op("v", "copy", reads=["pexp"], writes=["pn0"], out=pn[0][:, :],
                     in_=pexp[:, :])
                cur = 0
                for rnd in range(cfg.TOPK):
                    P.op("v", "red", reads=[f"pn{cur}"], writes=["rr3"], out=rr[:, 3:4],
                         in_=pn[cur][:, :], op=ALU.max)
                    if rnd < cfg.TOPK - 1:
                        P.op("v", "ts", reads=[f"pn{cur}", "rr3"], writes=["msk"],
                             out=msk[:, :], in0=pn[cur][:, :], s1=rr[:, 3:4],
                             op0=ALU.is_ge)
                        P.op("v", "stt", reads=["msk", f"pn{cur}"],
                             writes=[f"pn{1 - cur}"], out=pn[1 - cur][:, :],
                             in0=msk[:, :], s=-2.0, in1=pn[cur][:, :], op0=ALU.mult,
                             op1=ALU.add)
                        cur = 1 - cur
                P.op("v", "ts", reads=["pexp", "rr3"], writes=["mask_sb"], waw=False,
                     out=mask_sb[:, t_ * E:(t_ + 1) * E],
                     in0=pexp[:, :], s1=rr[:, 3:4], op0=ALU.is_ge)
                P.op("v", "tt", reads=["pexp", "mask_sb"], writes=["coef_sb"], waw=False,
                     out=coef_sb[:, t_ * E:(t_ + 1) * E], in0=pexp[:, :],
                     in1=mask_sb[:, t_ * E:(t_ + 1) * E], op=ALU.mult)
                kaop(["coef_sb"])
            for g in range(G):
                jc = (4 * g) // 16
                xrow = (4 * g) % 16
                for m in range(4):
                    srow = (xrow + m) * 8
                    P.op("g", "dma", reads=ar(a_q, a_k), writes=wb(a_QG, a_KG),
                         waw=False, dmasem="sq",
                         dst=wbig[32 * m:32 * m + 8, a_QG:a_QG + T],
                         src=arena[srow:srow + 8, a_q + jc * T:a_q + (jc + 1) * T])
                    P.op("g", "dma", reads=ar(a_k, a_v), writes=wb(a_KG, a_PB),
                         waw=False, dmasem="sk",
                         dst=wbig[32 * m:32 * m + 8, a_KG:a_KG + TX],
                         src=arena[srow:srow + 8, a_k + jc * TX:a_k + (jc + 1) * TX])
                    kaop(wb(a_QG, a_PB))
                for c in range(NKT):
                    cp = c % 2
                    if c == 0:
                        n0, Ncq = 128, 128
                    elif c == NKT - 1:
                        n0, Ncq = 0, 128
                    else:
                        n0, Ncq = 0, 256
                    q0 = (c - 1) * 128 + n0
                    for m in range(4):
                        mmop(pF[m][:, 0:Ncq],
                             wbig[32 * m:32 * m + 8, a_KG + c * 128:a_KG + c * 128 + 128],
                             wbig[32 * m:32 * m + 8, a_QG + q0:a_QG + q0 + Ncq],
                             True, True, tp=(32 * m, 0),
                             rd=wb(a_QG, a_PB), wr=[f"pF{m}"])
                    P.op("v", "copy", reads=ar(a_v, a_an), writes=[f"vaug{cp}"],
                         out=vaug[:, cp * 36:(cp + 1) * 36].rearrange(
                             "p (m d) -> p m d", d=9)[:, :, 0:8],
                         in_=arena[:, a_v + c * QR + 32 * g: a_v + c * QR + 32 * g + 32]
                         .rearrange("p (m d) -> p m d", d=8))
                    pbase = a_PB + cp * 1024
                    pres = wb(pbase, pbase + 1024)
                    P.op("a", "act",
                         reads=["pF0", "pF1", "pF2", "pF3", "cst"], writes=pres,
                         waw=False,
                         out=wbig[:, pbase:pbase + 1024].rearrange(
                             "p (m x) -> p m x", m=4)[:, :, n0:n0 + Ncq],
                         in_=pS4[:, :].rearrange("p (m x) -> p m x", m=4)[
                             :, :, 0:Ncq],
                         fn=AF.Exp, scale=cst[:, 1:2])
                    dtab, dres = (decay0_b, "decay0_b") if (c == 0 and h == 0) \
                        else (decay_b, "decay_b")
                    pview = wbig[:, pbase:pbase + 1024].rearrange(
                        "p (m n) -> p m n", m=4)[:, :, n0:n0 + Ncq]
                    P.op("v", "tt", reads=pres + [dres], writes=pres,
                         waw=False, out=pview, in0=pview,
                         in1=dtab[:, n0:n0 + Ncq].unsqueeze(1).broadcast_to(
                             [128, 4, Ncq]),
                         op=ALU.mult)
                    kaop(pres)
                    if c >= 1:
                        t_ = c - 1
                        ap_ = 4 + t_ % 2
                        for m in range(4):
                            for cc in (c - 1, c):
                                ccp = cc % 2
                                coloff = 128 if cc == c - 1 else 0
                                pb2 = a_PB + ccp * 1024
                                mmop(pF[ap_][32 * m:32 * m + 9, 0:128],
                                     vaug[:, ccp * 36 + m * 9: ccp * 36 + m * 9 + 9],
                                     wbig[:, pb2 + m * 256 + coloff:
                                          pb2 + m * 256 + coloff + 128],
                                     cc == c - 1, cc == c, tp=(0, 32 * m),
                                     rd=[f"vaug{ccp}"] + wb(pb2, pb2 + 1024),
                                     wr=[f"pF{ap_}"])
                        P.op("a", "act", reads=[f"pF{ap_}"], writes=["av_sb"],
                             out=av_sb[:, :], in_=pF[ap_][:, 0:128], fn=AF.Copy)
                        tf = t_ % 2
                        P.op("t", "tr", reads=["av_sb", "ident_f"], writes=[f"pF{tf}"],
                             out=pF[tf][:, 0:128], in_=av_sb[:, :], ident=ident_f[:, :])
                        P.op("v", "recip", reads=[f"pF{tf}"], writes=["rsum4"],
                             out=rsum4[:, :],
                             in_=pF[tf][:, 0:128].rearrange(
                                 "p (m r) -> p m r", r=32)[:, :, 8:9].rearrange(
                                 "p m r -> p (m r)"))
                        P.op("v", "tt", reads=[f"pF{tf}", "rsum4"], writes=["avn_t1"],
                             out=avn_t1[:, :].rearrange("p (m d) -> p m d", d=8),
                             in0=pF[tf][:, 0:128].rearrange(
                                 "p (m r) -> p m r", r=32)[:, :, 0:8],
                             in1=rsum4[:, :].unsqueeze(2).broadcast_to([128, 4, 8]),
                             op=ALU.mult)
                        P.op("v", "tt", reads=["avn_t1", "vembd_bc"],
                             writes=ar(a_an, a_af), waw=False,
                             out=arena[:, a_an + t_ * AVW + g * 32:
                                       a_an + t_ * AVW + g * 32 + 32],
                             in0=avn_t1[:, :], in1=vembd_bc[:, g * 32:g * 32 + 32],
                             op=ALU.mult)
            for t_ in range(TT):
                for cv in range(CV):
                    pp = (t_ * CV + cv) % 2
                    P.op("t", "tr", reads=ar(a_an, a_af) + ["ident_b"],
                         writes=[f"pT{pp}"], out=pT[pp][:, 0:128],
                         in_=arena[:, a_an + t_ * AVW + cv * 128:
                                   a_an + t_ * AVW + cv * 128 + 128],
                         ident=ident_b[:, :])
                    P.op("v", "copy", reads=[f"pT{pp}"], writes=ar(a_af, a_end),
                         waw=False,
                         out=arena[:, a_af + cv * T + t_ * 128:
                                   a_af + cv * T + t_ * 128 + 128],
                         in_=pT[pp][:, 0:128])

            P.op("a", "preload", fn=AF.Sigmoid)
            P.op("g", "dma", writes=wb(w_wo, w_endD), dmasem="lwo",
                 dst=wbig[:, w_wo:w_wo + CV * D].rearrange("p (c f) -> p c f", c=CV),
                 src=wo_in.rearrange("c p f -> p c f"))
            if h == 0:
                for sl in range(NSL):
                    for cv in range(CV):
                        mmop(pF[0][0:1, 0:SW], v0c_sb[:, cv:cv + 1],
                             wbig[:, w_wo + cv * D + sl * SW:
                                  w_wo + cv * D + sl * SW + SW],
                             cv == 0, cv == CV - 1,
                             rd=["v0c_sb"] + wb(w_wo, w_endD), wr=["pF0"])
                    P.op("v", "copy", reads=["pF0"], writes=["tmpB"],
                         out=tmpB[0:1, 0:SW], in_=pF[0][0:1, 0:SW])
                    mmop(pF[1][:, 0:SW], ones_row[0:1, 0:128],
                         tmpB[0:1, 0:SW], True, True,
                         rd=["ones_row", "tmpB"], wr=["pF1"])
                    P.op("v", "copy", reads=["pF1"], writes=["c0_bc"], waw=False,
                         out=c0_bc[:, sl * SW:(sl + 1) * SW], in_=pF[1][:, 0:SW])
            for sl in range(NSL):
                wgb = w_wgA if sl % 2 == 0 else w_wgB
                wgres = wb(wgb, wgb + DC * SW)
                P.op("g", "dma", writes=wgres, dmasem=f"lwg{sl % 2}",
                     dst=wbig[:, wgb:wgb + DC * SW].rearrange("p (c f) -> p c f", c=DC),
                     src=wg_in[sl].rearrange("c p f -> p c f"))
                for t_ in range(TT):
                    pw, pwr = pF[2 + t_ % 2], f"pF{2 + t_ % 2}"
                    pg, pgr = pF[t_ % 2], f"pF{t_ % 2}"
                    for c in range(DC):
                        mmop(pg[:, 0:SW],
                             xn_fm[:, xn_off(c, 128 + t_ * 128):
                                   xn_off(c, 128 + t_ * 128) + 128],
                             wbig[:, wgb + c * SW:wgb + (c + 1) * SW],
                             c == 0, c == DC - 1, rd=XF + wgres, wr=[pgr])
                    for cv in range(CV):
                        mmop(pw[:, 0:SW],
                             arena[:, a_af + cv * T + t_ * 128:
                                   a_af + cv * T + t_ * 128 + 128],
                             wbig[:, w_wo + cv * D + sl * SW:
                                  w_wo + cv * D + sl * SW + SW],
                             cv == 0, cv == CV - 1,
                             rd=ar(a_af, a_end) + wb(w_wo, w_endD), wr=[pwr])
                    P.op("a", "act", reads=[pgr], writes=["tmpA"],
                         out=tmpA[:, 0:SW], in_=pg[:, 0:SW], fn=AF.Sigmoid)
                    P.op("v", "tt", reads=[pwr, "c0_bc"], writes=["tmpB"],
                         out=tmpB[:, 0:SW], in0=pw[:, 0:SW],
                         in1=c0_bc[:, sl * SW:(sl + 1) * SW], op=ALU.add)
                    P.op("v", "tt", reads=["tmpA", "tmpB"], writes=[f"acc{t_}"],
                         waw=False,
                         out=acc[:, t_ * D + sl * SW: t_ * D + sl * SW + SW],
                         in0=tmpB[:, 0:SW], in1=tmpA[:, 0:SW], op=ALU.mult)

            if dump and h == 0:
                P.phase(f"dumpD{h}")
                dbg = nc.dram_tensor("dbg", [6, 128, max(DC * TX, TT * D)], F32,
                                     kind="ExternalOutput").ap()
                P.op("g", "dma", reads=XF, dmasem="dbg0",
                     dst=dbg[0, :, 0:DC * TX], src=xn_fm[:, :])
                P.op("g", "dma", reads=ar(a_q, a_an), dmasem="dbg1",
                     dst=dbg[1, :, 0:a_an], src=arena[:, 0:a_an])
                P.op("s", "dma", reads=["logits"], dmasem="dbg2",
                     dst=dbg[2, :, 0:TT * ER], src=logits[:, :])
                P.op("s", "dma", reads=["coef_sb"], dmasem="dbg3",
                     dst=dbg[3, :, 0:TT * E], src=coef_sb[:, :])
                P.op("g", "dma", reads=ar(a_an, a_end), dmasem="dbg4",
                     dst=dbg[4, :, 0:a_end - a_an], src=arena[:, a_an:a_end])
                P.op("s", "dma", reads=[f"acc{t}" for t in range(TT)], dmasem="dbg5",
                     dst=dbg[5, :, 0:TT * D], src=acc[:, :])
            P.op("a", "preload", fn=AF.Silu)

            BIG = float(1 << 20)
            for t_ in range(TT):
                P.op("v", "tts", reads=["mask_sb"], writes=["selr"], waw=False,
                     out=selr[:, t_ * E:(t_ + 1) * E],
                     in0=mask_sb[:, t_ * E:(t_ + 1) * E],
                     in1=mask_sb[:, t_ * E:(t_ + 1) * E])
            P.op("v", "tt", reads=["selr", "mask_sb"], writes=["selr"],
                 out=selr[:, :], in0=selr[:, :], in1=mask_sb[:, :], op=ALU.subtract)
            for nm, buf in (("slotacc", slotacc), ("selsum", selsum)):
                P.op("v", "memset", writes=[nm], out=buf[:, :], val=0.0)
            rview = lambda buf, r: buf[:, :].rearrange("p (t r) -> p r t", r=4)[:, r, :]
            eview = lambda buf, e: buf[:, :].rearrange("p (t e) -> p e t", e=E)[:, e, :]
            for e in range(E):
                me = eview(mask_sb, e)
                P.op("v", "tts", reads=["mask_sb"], writes=["scn"], out=scn[:, :],
                     in0=me, in1=me)
                P.op("v", "tt", reads=["scn", "mask_sb"], writes=["excl"],
                     out=excl[:, :], in0=scn[:, :], in1=me, op=ALU.subtract)
                P.op("v", "copy", reads=["scn"], writes=["rt_b"], out=rt_b[:, :],
                     in_=scn[:, TT - 1:TT])
                mmop(pF[5][:, 0:1], tri_b[:, :], rt_b[:, :], True, True,
                     rd=["tri_b", "rt_b"], wr=["pF5"])
                P.op("v", "ts", reads=["excl", "pF5"], writes=["posf"],
                     out=posf[:, :], in0=excl[:, :], s1=pF[5][:, 0:1], op0=ALU.add)
                P.op("v", "ts", reads=["posf"], writes=["msk"], out=msk[:, 0:TT],
                     in0=posf[:, :], s1=float(C), op0=ALU.is_ge)
                P.op("v", "stt", reads=["msk", "posf"], writes=["posf"],
                     out=posf[:, :], in0=msk[:, 0:TT], s=BIG, in1=posf[:, :],
                     op0=ALU.mult, op1=ALU.add)
                P.op("v", "ts", reads=["posf"], writes=["posf"], out=posf[:, :],
                     in0=posf[:, :], s1=float(e * C2 + h * C), op0=ALU.add)
                for r in range(4):
                    P.op("v", "ts", reads=["selr"], writes=["pn0"], out=pn[0][:, 0:TT],
                         in0=eview(selr, e), s1=float(r), op0=ALU.is_equal)
                    P.op("v", "tt", reads=["pn0", "mask_sb"], writes=["pn0"],
                         out=pn[0][:, 0:TT], in0=pn[0][:, 0:TT], in1=me, op=ALU.mult)
                    P.op("v", "tt", reads=["pn0", "posf"], writes=["pn1"],
                         out=pn[1][:, 0:TT], in0=pn[0][:, 0:TT], in1=posf[:, :],
                         op=ALU.mult)
                    P.op("v", "tt", reads=["pn1", "slotacc"], writes=["slotacc"],
                         out=rview(slotacc, r), in0=rview(slotacc, r),
                         in1=pn[1][:, 0:TT], op=ALU.add)
                    P.op("v", "tt", reads=["pn0", "selsum"], writes=["selsum"],
                         out=rview(selsum, r), in0=rview(selsum, r),
                         in1=pn[0][:, 0:TT], op=ALU.add)
            P.op("v", "ts", reads=["selsum"], writes=["selsum"], out=selsum[:, :],
                 in0=selsum[:, :], s1=-1.0, op0=ALU.add)
            P.op("v", "stt", reads=["selsum", "slotacc"], writes=["slotacc"],
                 out=slotacc[:, :], in0=selsum[:, :], s=-BIG, in1=slotacc[:, :],
                 op0=ALU.mult, op1=ALU.add)
            P.op("v", "copy", reads=["slotacc"], writes=["posI"], waw=False,
                 out=posI[:, h * TT * 4:(h + 1) * TT * 4], in_=slotacc[:, :])

            for t_ in range(TT):
                stg, stgr = (x_t, "x_t") if t_ % 2 == 0 else (xn_t, "xn_t")
                P.op("g", "dma", reads=["xn_dram"], writes=[stgr],
                     dmasem=f"lxs{t_ % 2}", dst=stg[:, 0:D],
                     src=xn_dram[128 + t_ * 128:128 + (t_ + 1) * 128, :])
                for r in range(4):
                    pcl = posI[:, h * TT * 4 + t_ * 4 + r:h * TT * 4 + t_ * 4 + r + 1]
                    P.op("g", "dma", reads=[stgr, "posI"], writes=["xgg"],
                         waw=False, dmasem=f"sxg{t_ % 2}", kindx=1,
                         dst=(xg_glob, IndirectOffsetOnAxis(pcl, 0)),
                         src=(stg[:, 0:D], None),
                         bc=E * C2 - 1, cop=None)
                    P.op("g", "dma", reads=["coef_sb", "posI"], writes=["gcg"],
                         waw=False, dmasem=f"sgc{t_ % 2}", kindx=1,
                         dst=(gc_glob, IndirectOffsetOnAxis(pcl, 0)),
                         src=(coef_sb[:, t_ * E:(t_ + 1) * E], None),
                         bc=E * C2 - 1, cop=None)
                kaop(["xgg", "gcg"])

            def gu_pass(srcgu, h_base, rhs_fn, chunks):
                for i in range(IC):
                    ggb = w_ggA if i % 2 == 0 else w_guA
                    gub = ggb + D
                    ggres = wb(ggb, ggb + D)
                    gures = wb(gub, gub + D)
                    P.op("g", "dma", writes=ggres + gures, dmasem=f"lgg{i % 2}",
                         dst=wbig[:, ggb:ggb + 2 * D], src=srcgu(i))
                    tokw = sum(n for _, n in chunks)
                    for s, (off, N) in enumerate(chunks):
                        k2 = (i * len(chunks) + s) % 2
                        pg_, pu_ = pF[2 * k2], pF[2 * k2 + 1]
                        pgr, pur = f"pF{2 * k2}", f"pF{2 * k2 + 1}"
                        for c in range(DC):
                            mmop(pg_[:, 0:N], wbig[:, ggb + c * 128:ggb + c * 128 + 128],
                                 rhs_fn(c, off, N),
                                 c == 0, c == DC - 1, rd=ggres + rhs_res, wr=[pgr])
                        for c in range(DC):
                            mmop(pu_[:, 0:N], wbig[:, gub + c * 128:gub + c * 128 + 128],
                                 rhs_fn(c, off, N),
                                 c == 0, c == DC - 1, rd=gures + rhs_res, wr=[pur])
                        tp_ = "tmpA" if s % 2 == 0 else "tmpB"
                        tpb = tmpA if s % 2 == 0 else tmpB
                        P.op("a", "act", reads=[pgr], writes=[tp_],
                             out=tpb[:, 0:N], in_=pg_[:, 0:N], fn=AF.Silu)
                        P.op("v", "tt", reads=[tp_, pur],
                             writes=ar(h_base, h_base + IC * tokw), waw=False,
                             out=arena[:, h_base + i * tokw + off:
                                       h_base + i * tokw + off + N],
                             in0=tpb[:, 0:N], in1=pu_[:, 0:N], op=ALU.mult)

            def down_pass(srcblk, tokw, h_base, n_tt, evac_fn):
                for fq in range(NSL):
                    dnb = w_dnA if fq % 2 == 0 else w_dnB
                    dres = wb(dnb, dnb + IC * SW)
                    P.op("g", "dma", writes=dres, dmasem=f"ldn{fq % 2}",
                         dst=wbig[:, dnb:dnb + IC * SW].rearrange(
                             "p (c f) -> p c f", c=IC),
                         src=srcblk(fq).rearrange("c p f -> p c f"))
                    for t_ in range(n_tt):
                        pd, pdr = pF[4 + t_ % 2], f"pF{4 + t_ % 2}"
                        for c in range(IC):
                            mmop(pd[:, 0:SW],
                                 arena[:, h_base + c * tokw + t_ * 128:
                                       h_base + c * tokw + t_ * 128 + 128],
                                 wbig[:, dnb + c * SW:dnb + (c + 1) * SW],
                                 c == 0, c == IC - 1,
                                 rd=ar(h_base, h_base + IC * tokw) + dres, wr=[pdr])
                        evac_fn(t_, fq, pd, pdr)

            rhs_res = XF
            xn_rhs = lambda c, off, N: xn_fm[:, xn_off(c, 128 + off):
                                             xn_off(c, 128 + off) + N]
            sh_chunks = [(s * TW, TW) for s in range(NTS)]

            def sh_evac(t_, fq, pd, pdr):
                P.op("v", "stt", reads=[pdr, f"acc{t_}", "sg_sb"],
                     writes=[f"acc{t_}"], waw=False,
                     out=acc[:, t_ * D + fq * SW: t_ * D + fq * SW + SW],
                     in0=pd[:, 0:SW], s=sg_sb[:, t_:t_ + 1],
                     in1=acc[:, t_ * D + fq * SW: t_ * D + fq * SW + SW],
                     op0=ALU.mult, op1=ALU.add)

            for hg in range(NHG):
                gu_pass(lambda i, hg=hg: gu_in[hg * IC + i], 0, xn_rhs, sh_chunks)
                down_pass(lambda fq, hg=hg: dn_in[hg, fq], T, 0, TT, sh_evac)

            if h == 0:
                for t_ in range(TT):
                    P.op("s", "dma", reads=[f"acc{t_}"], writes=["adr"], waw=False,
                         dmasem=f"aw{t_}",
                         dst=acc_dram[t_ * 128:(t_ + 1) * 128, :],
                         src=acc[:, t_ * D:(t_ + 1) * D])
                    kaop(["adr"])

        P.phase("expg")
        IH0 = max(0, (XN - DC * C2) // C2)
        IH0 = min(IH0, IC)
        h_res = ["xfB"] + ar(0, IC * T)

        def h_ap(i, s0, s1):
            if i < IH0:
                base = DC * C2 + i * C2
                return xn_fm[:, base + s0:base + s1]
            return arena[:, (i - IH0) * C2 + s0:(i - IH0) * C2 + s1]

        g_chunks = [(o, min(512, C2 - o)) for o in range(0, C2, 512)]

        def xg_rhs(c, off, N):
            return xn_fm[:, c * C2 + off:c * C2 + off + N]

        rhs_res = ["xfA"]

        def gu_pass2(srcgu):
            for i in range(IC):
                ggb = w_ggA if i % 2 == 0 else w_guA
                gub = ggb + D
                ggres, gures = wb(ggb, ggb + D), wb(gub, gub + D)
                P.op("g", "dma", writes=ggres + gures, dmasem=f"lgg{i % 2}",
                     dst=wbig[:, ggb:ggb + 2 * D], src=srcgu(i))
                for s, (off, N) in enumerate(g_chunks):
                    k2 = (i * len(g_chunks) + s) % 2
                    pg_, pu_ = pF[2 * k2], pF[2 * k2 + 1]
                    pgr, pur = f"pF{2 * k2}", f"pF{2 * k2 + 1}"
                    for c in range(DC):
                        mmop(pg_[:, 0:N], wbig[:, ggb + c * 128:ggb + c * 128 + 128],
                             xg_rhs(c, off, N), c == 0, c == DC - 1,
                             rd=ggres + ["xfA"], wr=[pgr])
                    for c in range(DC):
                        mmop(pu_[:, 0:N], wbig[:, gub + c * 128:gub + c * 128 + 128],
                             xg_rhs(c, off, N), c == 0, c == DC - 1,
                             rd=gures + ["xfA"], wr=[pur])
                    tp_ = "tmpA" if s % 2 == 0 else "tmpB"
                    tpb = tmpA if s % 2 == 0 else tmpB
                    P.op("a", "act", reads=[pgr], writes=[tp_],
                         out=tpb[:, 0:N], in_=pg_[:, 0:N], fn=AF.Silu)
                    P.op("v", "tt", reads=[tp_, pur], writes=h_res, waw=False,
                         out=h_ap(i, off, off + N),
                         in0=tpb[:, 0:N], in1=pu_[:, 0:N], op=ALU.mult)

        def stage(e):
            for i in range(CT2):
                if i % 2 == 0:
                    stgx, stgxr = xn_t[:, 0:D], ["xn_t"]
                else:
                    sb0 = (IC - IH0) * C2
                    stgx, stgxr = arena[:, sb0:sb0 + D], ar(sb0, sb0 + D)
                P.op("g", "dma", reads=["xgg"], writes=stgxr,
                     dmasem=f"lxg{i % 2}", dst=stgx,
                     src=xg_glob[e * C2 + i * 128:e * C2 + (i + 1) * 128, :])
                for c in range(DC):
                    pp = c % 2
                    P.op("t", "tr", reads=stgxr + ["ident_b"],
                         writes=[f"pT{pp}"], out=pT[pp][:, 0:128],
                         in_=stgx[:, c * 128:(c + 1) * 128],
                         ident=ident_b[:, :])
                    P.op("v", "copy", reads=[f"pT{pp}"],
                         writes=["xfA"], waw=False,
                         out=xn_fm[:, c * C2 + i * 128:c * C2 + i * 128 + 128],
                         in_=pT[pp][:, 0:128])
            P.op("g", "dma", reads=["gcg"], writes=[f"gcoef{e % 2}"],
                 dmasem=f"lgc{e % 2}",
                 dst=gcoef[e % 2][:, :].rearrange("p (i e) -> p i e", i=CT2),
                 src=gc_glob[e * C2:(e + 1) * C2].rearrange(
                     "(i p) e -> p i e", p=128))

        stage(0)
        for e in range(E):
            gu_pass2(lambda i, e=e: w13_in[e, i])
            if e + 1 < E:
                stage(e + 1)
            for fq in range(NSL):
                dnb = w_dnA if fq % 2 == 0 else w_dnB
                dres = wb(dnb, dnb + IC * SW)
                P.op("g", "dma", writes=dres, dmasem=f"ldn{fq % 2}",
                     dst=wbig[:, dnb:dnb + IC * SW].rearrange(
                         "p (c f) -> p c f", c=IC),
                     src=w2_in[e, fq].rearrange("c p f -> p c f"))
                for t_ in range(CT2):
                    pd, pdr = pF[4 + t_ % 2], f"pF{4 + t_ % 2}"
                    for c in range(IC):
                        mmop(pd[:, 0:SW], h_ap(c, t_ * 128, (t_ + 1) * 128),
                             wbig[:, dnb + c * SW:dnb + (c + 1) * SW],
                             c == 0, c == IC - 1, rd=h_res + dres, wr=[pdr])
                    tpb, tp_ = (tmpA, "tmpA") if (t_ * NSL + fq) % 2 == 0 \
                        else (tmpB, "tmpB")
                    P.op("v", "ts", reads=[pdr, f"gcoef{e % 2}"], writes=[tp_],
                         out=tpb[:, 0:SW], in0=pd[:, 0:SW],
                         s1=gcoef[e % 2][:, t_ * E + e:t_ * E + e + 1],
                         op0=ALU.mult)
                    P.op("s", "dma", reads=[tp_], writes=["ogg"], waw=False,
                         dmasem=f"sog{(t_ * NSL + fq) % 2}",
                         dst=og_glob[e * C2 + t_ * 128:e * C2 + (t_ + 1) * 128,
                                     fq * SW:(fq + 1) * SW],
                         src=tpb[:, 0:SW])

        for hh in (1, 0):
            if hh == 0:
                for t_ in range(TT):
                    P.op("s", "dma", reads=["adr"], writes=[f"acc{t_}"],
                         dmasem=f"ar{t_}", dst=acc[:, t_ * D:(t_ + 1) * D],
                         src=acc_dram[t_ * 128:(t_ + 1) * 128, :])
                    kaop([f"acc{t_}"])
            for r in range(4):
                for t_ in range(TT):
                    pcl = posI[:, hh * TT * 4 + t_ * 4 + r:
                               hh * TT * 4 + t_ * 4 + r + 1]
                    P.op("g", "dma", reads=["ogg", "posI"], writes=[f"acc{t_}"],
                         dmasem=f"sga{t_}", kindx=1,
                         dst=(acc[:, t_ * D:(t_ + 1) * D], None),
                         src=(og_glob, IndirectOffsetOnAxis(pcl, 0)),
                         bc=E * C2 - 1, cop=ALU.add)
                    kaop([f"acc{t_}"])
            NXC = max(1, D // TMPW)
            for t_ in range(TT):
                r0 = hh * T + 128 + t_ * 128
                for xh in range(NXC):
                    k = (t_ * NXC + xh) % 2
                    stg, stgr = (tmpA, "tmpA") if k == 0 else (tmpB, "tmpB")
                    cw = min(TMPW, D)
                    P.op("s", "dma", writes=[stgr], dmasem=f"dg{k}",
                         dst=stg[:, 0:cw],
                         src=xin[r0:r0 + 128, xh * cw:(xh + 1) * cw])
                    P.op("v", "tt", reads=[stgr, f"acc{t_}"], writes=[f"acc{t_}"],
                         waw=False,
                         out=acc[:, t_ * D + xh * cw:t_ * D + (xh + 1) * cw],
                         in0=acc[:, t_ * D + xh * cw:t_ * D + (xh + 1) * cw],
                         in1=stg[:, 0:cw], op=ALU.add)
                    kaop([f"acc{t_}"])
                P.op("s", "dma", reads=[f"acc{t_}"], dmasem=f"st{t_}",
                     dst=out[hh * T + t_ * 128:hh * T + t_ * 128 + 128, :],
                     src=acc[:, t_ * D:(t_ + 1) * D])

        P.finalize()
        sems = {k: es.enter_context(nc.semaphore("sm_" + k))
                for k in sorted(P.all_sems)}
        watermark = {}
        for name, ops in P.phases:
            if ops:
                emit_phase(nc, ops, sems, watermark)
        with nc.Block() as blk:
            def fb(eng):
                for k in sorted(P.all_sems):
                    if not k.startswith("E") and P.final_counts.get(k, 0) > 0:
                        eng.wait_ge(sems[k], P.final_counts[k])
            blk.sync(fb)
    return nc


def kernel(**inputs):
    cfg = REAL
    in_maps, (B, S) = prepare_inputs(cfg, inputs)
    nc = build_nc(cfg)
    res = run_bass_kernel_spmd(nc, in_maps, core_ids=list(range(len(in_maps))))
    outp = np.concatenate([res.results[k]["out"] for k in range(len(in_maps))], axis=0)
    return outp.reshape(B, S, cfg.D)


# version 12
# speedup vs baseline: 1.0108x; 1.0108x over previous
import contextlib
import numpy as np

import concourse.bass as bass
import concourse.mybir as mybir
from concourse.bass_utils import run_bass_kernel_spmd
from concourse.bass import IndirectOffsetOnAxis

F32 = mybir.dt.float32
BF16 = mybir.dt.bfloat16
AF = mybir.ActivationFunctionType
ALU = mybir.AluOpType
AX = mybir.AxisListType


class Cfg:
    def __init__(self, D=2048, T=1024, NH=32, E=16, I=1408, Is=5632, TOPK=4, SW=None):
        self.D, self.T, self.NH, self.E, self.I, self.Is = D, T, NH, E, I, Is
        self.QKB = 8
        self.TOPK = TOPK
        self.DC = D // 128
        self.TT = T // 128
        self.TX = T + 128
        self.NKT = self.TX // 128
        self.G = NH // 4
        self.QR = NH * self.QKB
        self.JC = max(1, self.QR // 128)
        self.CV = max(1, self.QR // 128)
        self.AVW = max(self.QR, 128)
        self.IC = I // 128
        self.IsC = Is // 128
        self.NHG = self.IsC // self.IC
        self.SW = SW or min(512, D)
        self.NSL = D // self.SW
        self.TW = min(512, T)
        self.NTS = T // self.TW
        self.ER = E + 1
        mean = T * TOPK // E
        self.C = 128 * ((3 * mean // 2 + 127) // 128)
        self.CT = self.C // 128
        self.C2 = 2 * self.C
        self.CT2 = self.C2 // 128


REAL = Cfg()


def arr_lhsT(W, fout, fin):
    jF, cK = fout // 128, fin // 128
    return np.ascontiguousarray(
        W.reshape(jF, 128, cK, 128).transpose(0, 3, 2, 1).reshape(jF, 128, fin))


def arr_rhs(W, fout, fin):
    cK = fin // 128
    return np.ascontiguousarray(W.reshape(fout, cK, 128).transpose(1, 2, 0))


def prepare_inputs(cfg, inp):
    g = lambda k: np.ascontiguousarray(np.asarray(inp[k], dtype=np.float32))
    D, T, E, I, Is, QR = cfg.D, cfg.T, cfg.E, cfg.I, cfg.Is, cfg.QR

    x = g("L0_attn_hidden_states")
    B, S, _ = x.shape
    x2 = x.reshape(B * S, D)

    Wq, Wk, Wv = g("L0_attn_Wq"), g("L0_attn_Wk"), g("L0_attn_Wv")
    Wo, Wg = g("L0_attn_Wo"), g("L0_attn_Wg")
    v0, v1 = g("L0_attn_v_emb0"), g("L0_attn_v_emb1")
    gw, sgw = g("L0_ffn_gate_w"), g("L0_ffn_sgate_w")
    w13, w2 = g("L0_ffn_w13"), g("L0_ffn_w2")
    sgu, sdw = g("L0_ffn_shared_gate_up_w"), g("L0_ffn_shared_down_w")

    QP = 128 * cfg.JC
    if QP != QR:
        Wq = np.concatenate([Wq, np.zeros((QP - QR, D), np.float32)], 0)
        Wk = np.concatenate([Wk, np.zeros((QP - QR, D), np.float32)], 0)
    VP = 128 * cfg.CV
    Wo_p = Wo
    if VP != QR:
        Wo_p = np.concatenate([Wo, np.zeros((D, VP - QR), np.float32)], 1)
        v0p = np.concatenate([v0, np.zeros(VP - QR, np.float32)])
    else:
        v0p = v0

    sh = {}
    sh["wq_a"] = arr_lhsT(Wq, QP, D)
    sh["wk_a"] = arr_lhsT(Wk, QP, D)
    sh["wv_a"] = arr_rhs(Wv, QR, D)
    sh["wr_a"] = arr_rhs(np.concatenate([gw, sgw], 0), E + 1, D)
    sh["wo_a"] = arr_rhs(Wo_p, D, VP)
    sh["wg_a"] = np.ascontiguousarray(
        Wg.reshape(cfg.NSL, cfg.SW, cfg.DC, 128).transpose(0, 2, 3, 1))
    gu_t = arr_lhsT(sgu, 2 * Is, D)
    IsC = cfg.IsC
    sh["gu_a"] = np.ascontiguousarray(np.stack(
        [np.concatenate([gu_t[j], gu_t[IsC + j]], axis=1) for j in range(IsC)]))
    sh["dn_a"] = np.ascontiguousarray(
        sdw.T.reshape(cfg.NHG, cfg.IC, 128, cfg.NSL, cfg.SW).transpose(0, 3, 1, 2, 4))
    w13_l = [arr_lhsT(w13[e], 2 * I, D) for e in range(E)]
    IC = cfg.IC
    sh["w13_a"] = np.ascontiguousarray(np.stack(
        [np.stack([np.concatenate([w13_l[e][j], w13_l[e][IC + j]], axis=1)
                   for j in range(IC)]) for e in range(E)]))
    sh["w2_a"] = np.ascontiguousarray(np.stack([
        w2[e].T.reshape(cfg.IC, 128, cfg.NSL, cfg.SW).transpose(2, 0, 1, 3)
        for e in range(E)]))
    sh["vemb_row"] = np.ascontiguousarray(np.concatenate([v0, v1]).reshape(1, 2 * QR))
    sh["vemb0c"] = np.ascontiguousarray(v0p.reshape(cfg.CV, 128).T)
    sh["ident"] = np.eye(128, dtype=np.float32)
    sh["tri"] = np.tril(np.ones((128, 128), np.float32), -1)

    n_i = np.arange(256)[None, :].astype(np.float64)
    p_i = np.arange(128)[:, None].astype(np.float64)
    dd = n_i - p_i
    dec = np.where(dd >= 0, np.exp(np.where(dd >= 0, dd * np.log(0.45), 0.0)), 0.0)
    dec = dec.astype(np.float32)
    sh["decay"] = dec

    TC = 2 * T
    in_maps = []
    for k in range(B * S // TC):
        g0 = k * TC
        xc = np.empty((TC + 128, D), np.float32)
        if g0 % S == 0:
            xc[:128] = 0.0
            dec0 = np.zeros_like(dec)
        else:
            xc[:128] = x2[g0 - 128:g0]
            dec0 = dec
        xc[128:] = x2[g0:g0 + TC]
        m = {"x": xc, "decay0": np.ascontiguousarray(dec0)}
        m.update(sh)
        in_maps.append(m)
    return in_maps, (B, S)


class _Op:
    __slots__ = ("eng", "kind", "args", "deps", "inc", "cnt", "semkey", "dma")

    def __init__(self, eng, kind, args, semkey, dma):
        self.eng, self.kind, self.args = eng, kind, args
        self.deps, self.inc, self.cnt = [], False, None
        self.semkey, self.dma = semkey, dma


class Plan:
    def __init__(self):
        self.phases = []
        self.cur = None
        self.res = {}
        self.all_sems = set()

    def phase(self, name):
        self.cur = []
        self.phases.append((name, self.cur))

    def op(self, eng, kind, reads=(), writes=(), waw=True, dmasem=None, **args):
        dma = kind == "dma"
        semkey = dmasem if dma else ("E" + eng)
        self.all_sems.add(semkey)
        o = _Op(eng, kind, args, semkey, dma)
        deps = []
        for r in reads:
            e = self.res.get(r)
            if e:
                deps.extend(e["w"].values())
        for w in writes:
            e = self.res.get(w)
            if e:
                deps.extend(e["r"].values())
                if waw:
                    deps.extend(e["w"].values())
        seen = set()
        for d in deps:
            if d is o or id(d) in seen:
                continue
            seen.add(id(d))
            if o.eng == "t" and d.eng == "t" and not d.dma:
                continue
            o.deps.append(d)
        rkey = semkey if dma else o.eng
        for r in reads:
            self.res.setdefault(r, {"w": {}, "r": {}})["r"][rkey] = o
        for w in writes:
            e = self.res.setdefault(w, {"w": {}, "r": {}})
            e["w"][rkey] = o
            e["r"] = {}
        self.cur.append(o)
        return o

    def finalize(self):
        for _, ops in self.phases:
            for o in ops:
                for d in o.deps:
                    if not d.dma:
                        d.inc = True
        cnt = {}
        for _, ops in self.phases:
            for o in ops:
                if o.dma:
                    cnt[o.semkey] = cnt.get(o.semkey, 0) + 16
                    o.cnt = cnt[o.semkey]
                elif o.inc:
                    cnt[o.semkey] = cnt.get(o.semkey, 0) + 1
                    o.cnt = cnt[o.semkey]
        self.final_counts = cnt


ENGSEC = {"t": "tensor", "v": "vector", "a": "scalar", "g": "gpsimd", "s": "sync"}


def _emit_op(eng, o, bregs=None):
    k, a = o.kind, o.args
    if k == "dma":
        if a.get("kindx"):
            dst, out_off = a["dst"]
            srcp, in_off = a["src"]
            bv = a["bc"]
            if bregs is not None:
                if bv not in bregs:
                    bregs[bv] = eng.to_reg(bv)
                bv = bregs[bv]
            return eng.indirect_dma_start(
                dst, out_off, srcp, in_off,
                bounds_check=bv, oob_is_err=False,
                compute_op=a["cop"] if a["cop"] is not None else ALU.bypass)
        return eng.dma_start(a["dst"], a["src"])
    if k == "tts":
        return eng.tensor_tensor_scan(a["out"], a["in0"], a["in1"], 0.0,
                                      ALU.add, ALU.bypass)
    if k == "mm":
        return eng.matmul(a["out"], a["lhsT"], a["rhs"], start=a["start"],
                          stop=a["stop"], tile_position=a.get("tp"))
    if k == "tr":
        return eng.transpose(a["out"], a["in_"], a["ident"])
    if k == "act":
        kw = {}
        for f in ("bias", "scale"):
            if f in a:
                kw[f] = a[f]
        if "accum" in a:
            kw["accum_out"] = a["accum"]
        return eng.activation(a["out"], a["in_"], a["fn"], **kw)
    if k == "preload":
        return eng.preload_act_table(a["fn"])
    if k == "tt":
        return eng.tensor_tensor(a["out"], a["in0"], a["in1"], a["op"])
    if k == "ts":
        return eng.tensor_scalar(a["out"], a["in0"], a["s1"], a.get("s2"), a["op0"],
                                 a.get("op1", ALU.bypass))
    if k == "stt":
        return eng.scalar_tensor_tensor(a["out"], a["in0"], a["s"], a["in1"],
                                        a["op0"], a["op1"])
    if k == "copy":
        return eng.tensor_copy(a["out"], a["in_"])
    if k == "red":
        return eng.tensor_reduce(a["out"], a["in_"], AX.X, a["op"],
                                 negate=a.get("negate", False))
    if k == "recip":
        return eng.reciprocal(a["out"], a["in_"])
    if k == "memset":
        return eng.memset(a["out"], a["val"])
    raise ValueError(k)


def emit_phase(nc, ops, sems, watermark):
    engs = [e for e in ("g", "s", "t", "v", "a") if any(o.eng == e for o in ops)]
    with nc.Block() as blk:
        for e in engs:
            def body(eng, e=e):
                wm = watermark.setdefault(e, {})
                bregs = {}
                for o in ops:
                    if o.eng != e:
                        continue
                    waits = {}
                    for d in o.deps:
                        if d.cnt is None:
                            continue
                        if d.cnt > waits.get(d.semkey, 0):
                            waits[d.semkey] = d.cnt
                    for k in sorted(waits):
                        v = waits[k]
                        if v > wm.get(k, 0):
                            eng.wait_ge(sems[k], v)
                            wm[k] = v
                    ins = _emit_op(eng, o, bregs)
                    if o.dma:
                        ins.then_inc(sems[o.semkey], 16)
                    elif o.inc:
                        ins.then_inc(sems[o.semkey], 1)
            getattr(blk, ENGSEC[e])(body)


def build_nc(cfg, dump=False, prezero=False):
    nc = bass.Bass("TRN2", target_bir_lowering=False)
    D, T, TX, DC, TT, NKT = cfg.D, cfg.T, cfg.TX, cfg.DC, cfg.TT, cfg.NKT
    G, JC, CV, QR, E, ER = cfg.G, cfg.JC, cfg.CV, cfg.QR, cfg.E, cfg.ER
    AVW = cfg.AVW
    IC, IsC, NHG = cfg.IC, cfg.IsC, cfg.NHG
    SW, NSL, TW, NTS = cfg.SW, cfg.NSL, cfg.TW, cfg.NTS
    QP = 128 * JC
    VP = 128 * CV

    xin = nc.dram_tensor("x", [2 * T + 128, D], F32, kind="ExternalInput").ap()
    dec_in = nc.dram_tensor("decay", [128, 256], F32, kind="ExternalInput").ap()
    dec0_in = nc.dram_tensor("decay0", [128, 256], F32, kind="ExternalInput").ap()
    ident_in = nc.dram_tensor("ident", [128, 128], F32, kind="ExternalInput").ap()
    wq_in = nc.dram_tensor("wq_a", [JC, 128, D], F32, kind="ExternalInput").ap()
    wk_in = nc.dram_tensor("wk_a", [JC, 128, D], F32, kind="ExternalInput").ap()
    wv_in = nc.dram_tensor("wv_a", [DC, 128, QR], F32, kind="ExternalInput").ap()
    wr_in = nc.dram_tensor("wr_a", [DC, 128, ER], F32, kind="ExternalInput").ap()
    wo_in = nc.dram_tensor("wo_a", [CV, 128, D], F32, kind="ExternalInput").ap()
    wg_in = nc.dram_tensor("wg_a", [NSL, DC, 128, SW], F32, kind="ExternalInput").ap()
    gu_in = nc.dram_tensor("gu_a", [IsC, 128, 2 * D], F32, kind="ExternalInput").ap()
    dn_in = nc.dram_tensor("dn_a", [NHG, NSL, IC, 128, SW], F32,
                           kind="ExternalInput").ap()
    w13_in = nc.dram_tensor("w13_a", [E, IC, 128, 2 * D], F32, kind="ExternalInput").ap()
    w2_in = nc.dram_tensor("w2_a", [E, NSL, IC, 128, SW], F32, kind="ExternalInput").ap()
    vrow_in = nc.dram_tensor("vemb_row", [1, 2 * QR], F32, kind="ExternalInput").ap()
    v0c_in = nc.dram_tensor("vemb0c", [128, CV], F32, kind="ExternalInput").ap()
    out = nc.dram_tensor("out", [2 * T, D], F32, kind="ExternalOutput").ap()
    tri_in = nc.dram_tensor("tri", [128, 128], F32, kind="ExternalInput").ap()
    C, CT = cfg.C, cfg.CT
    xn_dram = nc.dram_tensor("xn_dram", [cfg.TX, D], BF16, kind="Internal").ap()
    C2, CT2 = cfg.C2, cfg.CT2
    xg_glob = nc.dram_tensor("xg_glob", [E * C2, D], BF16, kind="Internal").ap()
    og_glob = nc.dram_tensor("og_glob", [E * C2, D], F32, kind="Internal").ap()
    gc_glob = nc.dram_tensor("gc_glob", [E * C2, E], F32, kind="Internal").ap()
    acc_dram = nc.dram_tensor("acc_dram", [T, D], F32, kind="Internal").ap()

    P = Plan()
    es = contextlib.ExitStack()
    with es:
        sb = lambda n, sh_, dt=BF16: es.enter_context(nc.sbuf_tensor(n, sh_, dt))
        XN = max(DC * TX, DC * cfg.C2)
        xn_fm = sb("xn_fm", [128, XN])
        acc = sb("acc", [128, TT * D], F32)
        c0_bc = sb("c0_bc", [128, D], F32)
        vembd_bc = sb("vembd_bc", [128, QR], F32)
        decay_b = sb("decay_b", [128, 256])
        decay0_b = sb("decay0_b", [128, 256])
        ident_f = sb("ident_f", [128, 128], F32)
        ident_b = sb("ident_b", [128, 128])
        ones_row = sb("ones_row", [1, 128], F32)
        stats = sb("stats", [128, 8], F32)
        coef_sb = sb("coef_sb", [128, TT * E], F32)
        sg_sb = sb("sg_sb", [128, TT], F32)
        logits = sb("logits", [128, TT * ER], F32)
        rsum4 = sb("rsum4", [128, 4], F32)
        av_sb = sb("av_sb", [128, 128], F32)
        avn_t1 = sb("avn_t1", [128, 32], F32)
        rr = sb("rr", [128, 8], F32)
        cst = sb("cst", [128, 2], F32)
        tri_b = sb("tri_b", [128, 128])
        mask_sb = sb("mask_sb", [128, 0], F32) if False else None
        pexp = sb("pexp", [128, E], F32)
        pn = [sb(f"pn{i}", [128, E], F32) for i in range(2)]
        msk = sb("msk", [128, E], F32)
        mask_sb = sb("mask_s", [128, TT * E], F32)
        posI = sb("posI", [128, 2 * TT * 4], mybir.dt.int32)
        scn = sb("scn", [128, TT], F32)
        excl = sb("excl", [128, TT], F32)
        posf = sb("posf", [128, TT], F32)
        rt_b = sb("rt_b", [128, 1])
        selr = sb("selr", [128, TT * E], F32)
        slotacc = sb("slotacc", [128, TT * 4], F32)
        selsum = sb("selsum", [128, TT * 4], F32)
        gcoef = sb("gcoef", [128, CT2 * E], F32)
        TMPW = max(SW, TW, min(512, cfg.C2))
        tmpA = sb("tmpA", [128, TMPW], F32)
        tmpB = sb("tmpB", [128, TMPW], F32)
        x_t = sb("x_t", [128, D], F32)
        xn_t = sb("xn_t", [128, D])
        vaug = sb("vaug", [128, 2 * 36])
        v0c_sb = sb("v0c_sb", [128, CV])

        a_q, a_k = 0, JC * T
        a_v = a_k + JC * TX
        a_an = a_v + NKT * QR
        a_af = a_an + TT * AVW
        a_end = a_af + CV * T
        f_xgt = 0
        f_xgf = D
        f_h = f_xgf + DC * C
        AR1 = max(a_end, IC * T, f_h + IC * C)
        arena = sb("arena", [128, AR1])
        ar_cuts = sorted({0, a_k, a_v, a_an, a_af, a_end, IC * T, AR1,
                          f_xgf, f_h, f_h + IC * C})

        w_wq, w_wk = 0, JC * D
        w_wv = w_wk + JC * D
        w_wr = w_wv + DC * QR
        w_endB = w_wr + DC * ER
        w_wgA, w_wgB = 0, DC * SW
        w_wo = 2 * DC * SW
        w_endD = w_wo + CV * D
        w_ggA, w_ggB = 0, D
        w_guA, w_guB = 2 * D, 3 * D
        w_dnA = 4 * D
        w_dnB = w_dnA + IC * SW
        w_endF = w_dnB + IC * SW
        a_QG = 0
        a_KG = a_QG + T
        a_PB = a_KG + TX
        w_endC = a_PB + 2 * 1024
        a_QG2 = w_endC
        a_KG2 = a_QG2 + T
        w_endC2 = a_KG2 + TX
        WB = max(w_endB, w_endD, w_endF, w_endC2)
        wbig = sb("wbig", [128, WB])
        wb_cuts = sorted({0, w_wk, w_wv, w_wr, w_endB, w_wgB, w_wo, w_endD,
                          w_ggB, w_guA, w_guB, w_dnA, w_dnB, w_endF, WB,
                          a_KG, a_PB, a_PB + 1024, w_endC, a_KG2, w_endC2})

        def _regs(cuts, pfx, c0, c1):
            names = []
            for i in range(len(cuts) - 1):
                if cuts[i] < c1 and cuts[i + 1] > c0:
                    names.append(f"{pfx}{i}")
            return names

        ar = lambda c0, c1: _regs(ar_cuts, "AR", c0, c1)
        wb = lambda c0, c1: _regs(wb_cuts, "WB", c0, c1)

        ps = lambda n, sh_, dt=F32: es.enter_context(nc.psum_tensor(n, sh_, dt))
        pT = [ps(f"pT{i}", [128, 1024], BF16) for i in range(2)]
        pS4 = ps("pS4", [128, 2048])
        _px = [ps(f"pX{i}", [128, 512]) for i in range(2)]
        pF = [pS4[:, 512 * i:512 * (i + 1)] for i in range(4)] + _px

        mmop = lambda o, l, r, s0, s1, tp=None, rd=(), wr=(): P.op(
            "t", "mm", reads=rd, writes=wr, out=o, lhsT=l, rhs=r,
            start=s0, stop=s1, tp=tp)
        xn_off = lambda c, tok: c * TX + tok

        P.phase("consts")
        P.op("g", "memset", writes=["vaug"], out=vaug[:, :], val=1.0)
        P.op("g", "memset", writes=["cst"], out=cst[:, 0:1], val=1e-5)
        P.op("g", "memset", writes=["cst"], waw=False, out=cst[:, 1:2], val=0.125)
        P.op("g", "memset", writes=ar(a_an, a_af), out=arena[:, a_an:a_af], val=0.0)
        P.op("v", "memset", writes=["pF4"], out=pF[4][:, :], val=0.0)
        P.op("v", "memset", writes=["pF5"], out=pF[5][:, :], val=0.0)
        P.op("s", "dma", writes=["ident_f"], dmasem="dc0", dst=ident_f[:, :], src=ident_in)
        P.op("s", "dma", writes=["tmpA"], dmasem="dc1", dst=tmpA[0:1, 0:2 * QR],
             src=vrow_in)
        P.op("g", "dma", writes=["decay_b"], dmasem="dd0", dst=decay_b[:, :], src=dec_in)
        P.op("g", "dma", writes=["decay0_b"], dmasem="dd1", dst=decay0_b[:, :],
             src=dec0_in)
        P.op("g", "dma", writes=["v0c_sb"], dmasem="dd2", dst=v0c_sb[:, :], src=v0c_in)
        P.op("v", "copy", reads=["ident_f"], writes=["ident_b"], out=ident_b[:, :],
             in_=ident_f[:, :])
        P.op("v", "memset", writes=["ones_row"], out=ones_row[:, :], val=1.0)
        P.op("v", "tt", reads=["tmpA"], writes=["tmpB"], out=tmpB[0:1, 0:QR],
             in0=tmpA[0:1, QR:2 * QR], in1=tmpA[0:1, 0:QR], op=ALU.subtract)
        mmop(pF[0][:, 0:QR], ones_row[0:1, 0:128], tmpB[0:1, 0:QR], True, True,
             rd=["ones_row", "tmpB"], wr=["pF0"])
        P.op("v", "copy", reads=["pF0"], writes=["vembd_bc"], out=vembd_bc[:, :],
             in_=pF[0][:, 0:QR])
        P.op("g", "dma", writes=["tri_b"], dmasem="dtr", dst=tri_b[:, :], src=tri_in)
        P.op("g", "memset", writes=["x_t"], out=x_t[:, :], val=0.0)

        if prezero:
            for i in range(E * CT2):
                P.op("g", "dma", reads=["x_t"], writes=["xgg"], waw=False, dmasem="pz",
                     dst=xg_glob[i * 128:(i + 1) * 128, :], src=x_t[:, 0:D])
                P.op("g", "dma", reads=["x_t"], writes=["gcg"], waw=False, dmasem="pz",
                     dst=gc_glob[i * 128:(i + 1) * 128, :], src=x_t[:, 0:E])

        for h in range(2):
            half0 = h * T

            P.phase(f"ln{h}")
            P.op("a", "preload", fn=AF.Sqrt)
            for kt in range(NKT):
                r0 = half0 + kt * 128
                P.op("s", "dma", writes=["x_t"], dmasem="dx", dst=x_t[:, :],
                     src=xin[r0:r0 + 128, :])
                ST = stats
                P.op("v", "red", reads=["x_t"], writes=["s1"], out=ST[:, 0:1],
                     in_=x_t[:, :], op=ALU.add)
                P.op("a", "act", reads=["x_t"], writes=["xn_t", "ssq"],
                     out=xn_t[:, :], in_=x_t[:, :], fn=AF.Square, accum=ST[:, 1:2])
                P.op("v", "ts", reads=["s1"], writes=["mv"], out=ST[:, 2:3],
                     in0=ST[:, 0:1], s1=1.0 / D, op0=ALU.mult)
                P.op("v", "tt", reads=["mv"], writes=["msq"], out=ST[:, 3:4],
                     in0=ST[:, 2:3], in1=ST[:, 2:3], op=ALU.mult)
                P.op("v", "stt", reads=["ssq", "msq"], writes=["var"], out=ST[:, 4:5],
                     in0=ST[:, 1:2], s=1.0 / D, in1=ST[:, 3:4], op0=ALU.mult,
                     op1=ALU.subtract)
                P.op("a", "act", reads=["var", "cst"], writes=["std"], out=ST[:, 5:6],
                     in_=ST[:, 4:5], fn=AF.Sqrt, bias=cst[:, 0:1])
                P.op("v", "recip", reads=["std"], writes=["rstd"], out=ST[:, 6:7],
                     in_=ST[:, 5:6])
                P.op("v", "ts", reads=["x_t", "mv", "rstd"], writes=["xn_t"],
                     out=xn_t[:, :], in0=x_t[:, :], s1=ST[:, 2:3], s2=ST[:, 6:7],
                     op0=ALU.subtract, op1=ALU.mult)
                P.op("s", "dma", reads=["xn_t"], writes=["xn_dram"], dmasem="xnw",
                     dst=xn_dram[kt * 128:(kt + 1) * 128, :], src=xn_t[:, :])
                for c in range(DC):
                    pp = c % 2
                    P.op("t", "tr", reads=["xn_t", "ident_b"], writes=[f"pT{pp}"],
                         out=pT[pp][:, 0:128], in_=xn_t[:, c * 128:(c + 1) * 128],
                         ident=ident_b[:, :])
                    P.op("v", "copy", reads=[f"pT{pp}"], writes=["xn_fm"], waw=False,
                         out=xn_fm[:, xn_off(c, kt * 128):xn_off(c, kt * 128) + 128],
                         in_=pT[pp][:, 0:128])

            P.op("a", "preload", fn=AF.Sigmoid)
            P.op("g", "dma", writes=wb(w_wq, w_wk), dmasem="lwq",
                 dst=wbig[:, w_wq:w_wq + JC * D].rearrange("p (j f) -> p j f", j=JC),
                 src=wq_in.rearrange("j p f -> p j f"))
            P.op("g", "dma", writes=wb(w_wk, w_wv), dmasem="lwk",
                 dst=wbig[:, w_wk:w_wk + JC * D].rearrange("p (j f) -> p j f", j=JC),
                 src=wk_in.rearrange("j p f -> p j f"))
            P.op("g", "dma", writes=wb(w_wv, w_wr), dmasem="lwv",
                 dst=wbig[:, w_wv:w_wv + DC * QR].rearrange("p (c f) -> p c f", c=DC),
                 src=wv_in.rearrange("c p f -> p c f"))
            P.op("g", "dma", writes=wb(w_wr, w_endB), dmasem="lwr",
                 dst=wbig[:, w_wr:w_wr + DC * ER].rearrange("p (c f) -> p c f", c=DC),
                 src=wr_in.rearrange("c p f -> p c f"))
            for which in ("q", "k"):
                base = w_wq if which == "q" else w_wk
                wres = wb(base, base + JC * D)
                if which == "q":
                    offs = [(128 + s * TW, TW) for s in range(NTS)]
                else:
                    offs = [(s * TW, TW) for s in range(NTS)] + [(T, 128)]
                step = 0
                for jc in range(JC):
                    for off, N in offs:
                        pb, pbr = pF[step % 2], f"pF{step % 2}"
                        step += 1
                        for c in range(DC):
                            mmop(pb[:, 0:N],
                                 wbig[:, base + jc * D + c * 128:
                                      base + jc * D + c * 128 + 128],
                                 xn_fm[:, xn_off(c, off):xn_off(c, off) + N],
                                 c == 0, c == DC - 1, rd=wres + ["xn_fm"], wr=[pbr])
                        if which == "q":
                            dcol = a_q + jc * T + (off - 128)
                            dres = ar(a_q, a_k)
                        else:
                            dcol = a_k + jc * TX + off
                            dres = ar(a_k, a_v)
                        P.op("a", "act", reads=[pbr], writes=dres, waw=False,
                             out=arena[:, dcol:dcol + N], in_=pb[:, 0:N], fn=AF.Tanh)
            for ckt in range(NKT):
                pv, pvr = pF[2 + ckt % 2], f"pF{2 + ckt % 2}"
                pr, prr = pF[4 + ckt % 2], f"pF{4 + ckt % 2}"
                for c in range(DC):
                    lh = xn_fm[:, xn_off(c, ckt * 128):xn_off(c, ckt * 128) + 128]
                    mmop(pv[:, 0:QR], lh, wbig[:, w_wv + c * QR:w_wv + (c + 1) * QR],
                         c == 0, c == DC - 1, rd=["xn_fm"] + wb(w_wv, w_wr), wr=[pvr])
                    if ckt >= 1:
                        mmop(pr[:, 0:ER], lh, wbig[:, w_wr + c * ER:w_wr + (c + 1) * ER],
                             c == 0, c == DC - 1, rd=["xn_fm"] + wb(w_wr, w_endB),
                             wr=[prr])
                P.op("a", "act", reads=[pvr], writes=ar(a_v, a_an), waw=False,
                     out=arena[:, a_v + ckt * QR: a_v + (ckt + 1) * QR],
                     in_=pv[:, 0:QR], fn=AF.Sigmoid)
                if ckt >= 1:
                    t_ = ckt - 1
                    P.op("v", "copy", reads=[prr], writes=["logits"], waw=False,
                         out=logits[:, t_ * ER:(t_ + 1) * ER], in_=pr[:, 0:ER])
                    P.op("a", "act", reads=[prr], writes=["sg_sb"], waw=False,
                         out=sg_sb[:, t_:t_ + 1], in_=pr[:, E:E + 1], fn=AF.Sigmoid)

            P.op("a", "preload", fn=AF.Exp)
            for t_ in range(TT):
                L = logits[:, t_ * ER: t_ * ER + E]
                P.op("v", "red", reads=["logits"], writes=["rr0"], out=rr[:, 0:1],
                     in_=L, op=ALU.max, negate=True)
                P.op("a", "act", reads=["rr0", "logits"], writes=["pexp"],
                     out=pexp[:, :], in_=L, fn=AF.Exp, bias=rr[:, 0:1])
                P.op("v", "red", reads=["pexp"], writes=["rr1"], out=rr[:, 1:2],
                     in_=pexp[:, :], op=ALU.add)
                P.op("v", "recip", reads=["rr1"], writes=["rr2"], out=rr[:, 2:3],
                     in_=rr[:, 1:2])
                P.op("v", "ts", reads=["pexp", "rr2"], writes=["pexp"], out=pexp[:, :],
                     in0=pexp[:, :], s1=rr[:, 2:3], op0=ALU.mult)
                P.op("v", "copy", reads=["pexp"], writes=["pn0"], out=pn[0][:, :],
                     in_=pexp[:, :])
                cur = 0
                for rnd in range(cfg.TOPK):
                    P.op("v", "red", reads=[f"pn{cur}"], writes=["rr3"], out=rr[:, 3:4],
                         in_=pn[cur][:, :], op=ALU.max)
                    if rnd < cfg.TOPK - 1:
                        P.op("v", "ts", reads=[f"pn{cur}", "rr3"], writes=["msk"],
                             out=msk[:, :], in0=pn[cur][:, :], s1=rr[:, 3:4],
                             op0=ALU.is_ge)
                        P.op("v", "stt", reads=["msk", f"pn{cur}"],
                             writes=[f"pn{1 - cur}"], out=pn[1 - cur][:, :],
                             in0=msk[:, :], s=-2.0, in1=pn[cur][:, :], op0=ALU.mult,
                             op1=ALU.add)
                        cur = 1 - cur
                P.op("v", "ts", reads=["pexp", "rr3"], writes=["mask_sb"], waw=False,
                     out=mask_sb[:, t_ * E:(t_ + 1) * E],
                     in0=pexp[:, :], s1=rr[:, 3:4], op0=ALU.is_ge)
                P.op("v", "tt", reads=["pexp", "mask_sb"], writes=["coef_sb"], waw=False,
                     out=coef_sb[:, t_ * E:(t_ + 1) * E], in0=pexp[:, :],
                     in1=mask_sb[:, t_ * E:(t_ + 1) * E], op=ALU.mult)
            for g in range(G):
                jc = (4 * g) // 16
                xrow = (4 * g) % 16
                qb = a_QG if g % 2 == 0 else a_QG2
                kb = qb + T
                for m in range(4):
                    srow = (xrow + m) * 8
                    P.op("g", "dma", reads=ar(a_q, a_k), writes=wb(qb, kb),
                         waw=False, dmasem=f"sq{g % 2}",
                         dst=wbig[32 * m:32 * m + 8, qb:qb + T],
                         src=arena[srow:srow + 8, a_q + jc * T:a_q + (jc + 1) * T])
                    P.op("g", "dma", reads=ar(a_k, a_v), writes=wb(kb, kb + TX),
                         waw=False, dmasem=f"sk{g % 2}",
                         dst=wbig[32 * m:32 * m + 8, kb:kb + TX],
                         src=arena[srow:srow + 8, a_k + jc * TX:a_k + (jc + 1) * TX])
                for c in range(NKT):
                    cp = c % 2
                    if c == 0:
                        n0, Ncq = 128, 128
                    elif c == NKT - 1:
                        n0, Ncq = 0, 128
                    else:
                        n0, Ncq = 0, 256
                    q0 = (c - 1) * 128 + n0
                    for m in range(4):
                        mmop(pF[m][:, 0:Ncq],
                             wbig[32 * m:32 * m + 8, kb + c * 128:kb + c * 128 + 128],
                             wbig[32 * m:32 * m + 8, qb + q0:qb + q0 + Ncq],
                             True, True, tp=(32 * m, 0),
                             rd=wb(qb, kb + TX), wr=[f"pF{m}"])
                    P.op("v", "copy", reads=ar(a_v, a_an), writes=[f"vaug{cp}"],
                         out=vaug[:, cp * 36:(cp + 1) * 36].rearrange(
                             "p (m d) -> p m d", d=9)[:, :, 0:8],
                         in_=arena[:, a_v + c * QR + 32 * g: a_v + c * QR + 32 * g + 32]
                         .rearrange("p (m d) -> p m d", d=8))
                    pbase = a_PB + cp * 1024
                    pres = wb(pbase, pbase + 1024)
                    P.op("a", "act",
                         reads=["pF0", "pF1", "pF2", "pF3", "cst"], writes=pres,
                         waw=False,
                         out=wbig[:, pbase:pbase + 1024].rearrange(
                             "p (m x) -> p m x", m=4)[:, :, n0:n0 + Ncq],
                         in_=pS4[:, :].rearrange("p (m x) -> p m x", m=4)[
                             :, :, 0:Ncq],
                         fn=AF.Exp, scale=cst[:, 1:2])
                    dtab, dres = (decay0_b, "decay0_b") if (c == 0 and h == 0) \
                        else (decay_b, "decay_b")
                    pview = wbig[:, pbase:pbase + 1024].rearrange(
                        "p (m n) -> p m n", m=4)[:, :, n0:n0 + Ncq]
                    P.op("v", "tt", reads=pres + [dres], writes=pres,
                         waw=False, out=pview, in0=pview,
                         in1=dtab[:, n0:n0 + Ncq].unsqueeze(1).broadcast_to(
                             [128, 4, Ncq]),
                         op=ALU.mult)
                    if c >= 1:
                        t_ = c - 1
                        ap_ = 4 + t_ % 2
                        for m in range(4):
                            for cc in (c - 1, c):
                                ccp = cc % 2
                                coloff = 128 if cc == c - 1 else 0
                                pb2 = a_PB + ccp * 1024
                                mmop(pF[ap_][32 * m:32 * m + 9, 0:128],
                                     vaug[:, ccp * 36 + m * 9: ccp * 36 + m * 9 + 9],
                                     wbig[:, pb2 + m * 256 + coloff:
                                          pb2 + m * 256 + coloff + 128],
                                     cc == c - 1, cc == c, tp=(0, 32 * m),
                                     rd=[f"vaug{ccp}"] + wb(pb2, pb2 + 1024),
                                     wr=[f"pF{ap_}"])
                        P.op("a", "act", reads=[f"pF{ap_}"], writes=["av_sb"],
                             out=av_sb[:, :], in_=pF[ap_][:, 0:128], fn=AF.Copy)
                        tf = t_ % 2
                        P.op("t", "tr", reads=["av_sb", "ident_f"], writes=[f"pF{tf}"],
                             out=pF[tf][:, 0:128], in_=av_sb[:, :], ident=ident_f[:, :])
                        P.op("v", "recip", reads=[f"pF{tf}"], writes=["rsum4"],
                             out=rsum4[:, :],
                             in_=pF[tf][:, 0:128].rearrange(
                                 "p (m r) -> p m r", r=32)[:, :, 8:9].rearrange(
                                 "p m r -> p (m r)"))
                        P.op("v", "tt", reads=[f"pF{tf}", "rsum4"], writes=["avn_t1"],
                             out=avn_t1[:, :].rearrange("p (m d) -> p m d", d=8),
                             in0=pF[tf][:, 0:128].rearrange(
                                 "p (m r) -> p m r", r=32)[:, :, 0:8],
                             in1=rsum4[:, :].unsqueeze(2).broadcast_to([128, 4, 8]),
                             op=ALU.mult)
                        P.op("v", "tt", reads=["avn_t1", "vembd_bc"],
                             writes=ar(a_an, a_af), waw=False,
                             out=arena[:, a_an + t_ * AVW + g * 32:
                                       a_an + t_ * AVW + g * 32 + 32],
                             in0=avn_t1[:, :], in1=vembd_bc[:, g * 32:g * 32 + 32],
                             op=ALU.mult)
            for t_ in range(TT):
                for cv in range(CV):
                    pp = (t_ * CV + cv) % 2
                    P.op("t", "tr", reads=ar(a_an, a_af) + ["ident_b"],
                         writes=[f"pT{pp}"], out=pT[pp][:, 0:128],
                         in_=arena[:, a_an + t_ * AVW + cv * 128:
                                   a_an + t_ * AVW + cv * 128 + 128],
                         ident=ident_b[:, :])
                    P.op("v", "copy", reads=[f"pT{pp}"], writes=ar(a_af, a_end),
                         waw=False,
                         out=arena[:, a_af + cv * T + t_ * 128:
                                   a_af + cv * T + t_ * 128 + 128],
                         in_=pT[pp][:, 0:128])

            P.op("a", "preload", fn=AF.Sigmoid)
            P.op("g", "dma", writes=wb(w_wo, w_endD), dmasem="lwo",
                 dst=wbig[:, w_wo:w_wo + CV * D].rearrange("p (c f) -> p c f", c=CV),
                 src=wo_in.rearrange("c p f -> p c f"))
            if h == 0:
                for sl in range(NSL):
                    for cv in range(CV):
                        mmop(pF[0][0:1, 0:SW], v0c_sb[:, cv:cv + 1],
                             wbig[:, w_wo + cv * D + sl * SW:
                                  w_wo + cv * D + sl * SW + SW],
                             cv == 0, cv == CV - 1,
                             rd=["v0c_sb"] + wb(w_wo, w_endD), wr=["pF0"])
                    P.op("v", "copy", reads=["pF0"], writes=["tmpB"],
                         out=tmpB[0:1, 0:SW], in_=pF[0][0:1, 0:SW])
                    mmop(pF[1][:, 0:SW], ones_row[0:1, 0:128],
                         tmpB[0:1, 0:SW], True, True,
                         rd=["ones_row", "tmpB"], wr=["pF1"])
                    P.op("v", "copy", reads=["pF1"], writes=["c0_bc"], waw=False,
                         out=c0_bc[:, sl * SW:(sl + 1) * SW], in_=pF[1][:, 0:SW])
            for sl in range(NSL):
                wgb = w_wgA if sl % 2 == 0 else w_wgB
                wgres = wb(wgb, wgb + DC * SW)
                P.op("g", "dma", writes=wgres, dmasem=f"lwg{sl % 2}",
                     dst=wbig[:, wgb:wgb + DC * SW].rearrange("p (c f) -> p c f", c=DC),
                     src=wg_in[sl].rearrange("c p f -> p c f"))
                for t_ in range(TT):
                    pw, pwr = pF[2 + t_ % 2], f"pF{2 + t_ % 2}"
                    pg, pgr = pF[t_ % 2], f"pF{t_ % 2}"
                    for c in range(DC):
                        mmop(pg[:, 0:SW],
                             xn_fm[:, xn_off(c, 128 + t_ * 128):
                                   xn_off(c, 128 + t_ * 128) + 128],
                             wbig[:, wgb + c * SW:wgb + (c + 1) * SW],
                             c == 0, c == DC - 1, rd=["xn_fm"] + wgres, wr=[pgr])
                    for cv in range(CV):
                        mmop(pw[:, 0:SW],
                             arena[:, a_af + cv * T + t_ * 128:
                                   a_af + cv * T + t_ * 128 + 128],
                             wbig[:, w_wo + cv * D + sl * SW:
                                  w_wo + cv * D + sl * SW + SW],
                             cv == 0, cv == CV - 1,
                             rd=ar(a_af, a_end) + wb(w_wo, w_endD), wr=[pwr])
                    P.op("a", "act", reads=[pgr], writes=["tmpA"],
                         out=tmpA[:, 0:SW], in_=pg[:, 0:SW], fn=AF.Sigmoid)
                    P.op("v", "tt", reads=[pwr, "c0_bc"], writes=["tmpB"],
                         out=tmpB[:, 0:SW], in0=pw[:, 0:SW],
                         in1=c0_bc[:, sl * SW:(sl + 1) * SW], op=ALU.add)
                    P.op("v", "tt", reads=["tmpA", "tmpB"], writes=[f"acc{t_}"],
                         waw=False,
                         out=acc[:, t_ * D + sl * SW: t_ * D + sl * SW + SW],
                         in0=tmpB[:, 0:SW], in1=tmpA[:, 0:SW], op=ALU.mult)

            if dump and h == 0:
                P.phase(f"dumpD{h}")
                dbg = nc.dram_tensor("dbg", [6, 128, max(DC * TX, TT * D)], F32,
                                     kind="ExternalOutput").ap()
                P.op("g", "dma", reads=["xn_fm"], dmasem="dbg0",
                     dst=dbg[0, :, 0:DC * TX], src=xn_fm[:, :])
                P.op("g", "dma", reads=ar(a_q, a_an), dmasem="dbg1",
                     dst=dbg[1, :, 0:a_an], src=arena[:, 0:a_an])
                P.op("s", "dma", reads=["logits"], dmasem="dbg2",
                     dst=dbg[2, :, 0:TT * ER], src=logits[:, :])
                P.op("s", "dma", reads=["coef_sb"], dmasem="dbg3",
                     dst=dbg[3, :, 0:TT * E], src=coef_sb[:, :])
                P.op("g", "dma", reads=ar(a_an, a_end), dmasem="dbg4",
                     dst=dbg[4, :, 0:a_end - a_an], src=arena[:, a_an:a_end])
                P.op("s", "dma", reads=[f"acc{t}" for t in range(TT)], dmasem="dbg5",
                     dst=dbg[5, :, 0:TT * D], src=acc[:, :])
            P.op("a", "preload", fn=AF.Silu)

            BIG = float(1 << 20)
            for t_ in range(TT):
                P.op("v", "tts", reads=["mask_sb"], writes=["selr"], waw=False,
                     out=selr[:, t_ * E:(t_ + 1) * E],
                     in0=mask_sb[:, t_ * E:(t_ + 1) * E],
                     in1=mask_sb[:, t_ * E:(t_ + 1) * E])
            P.op("v", "tt", reads=["selr", "mask_sb"], writes=["selr"],
                 out=selr[:, :], in0=selr[:, :], in1=mask_sb[:, :], op=ALU.subtract)
            for nm, buf in (("slotacc", slotacc), ("selsum", selsum)):
                P.op("v", "memset", writes=[nm], out=buf[:, :], val=0.0)
            rview = lambda buf, r: buf[:, :].rearrange("p (t r) -> p r t", r=4)[:, r, :]
            eview = lambda buf, e: buf[:, :].rearrange("p (t e) -> p e t", e=E)[:, e, :]
            for e in range(E):
                me = eview(mask_sb, e)
                P.op("v", "tts", reads=["mask_sb"], writes=["scn"], out=scn[:, :],
                     in0=me, in1=me)
                P.op("v", "tt", reads=["scn", "mask_sb"], writes=["excl"],
                     out=excl[:, :], in0=scn[:, :], in1=me, op=ALU.subtract)
                P.op("v", "copy", reads=["scn"], writes=["rt_b"], out=rt_b[:, :],
                     in_=scn[:, TT - 1:TT])
                mmop(pF[5][:, 0:1], tri_b[:, :], rt_b[:, :], True, True,
                     rd=["tri_b", "rt_b"], wr=["pF5"])
                P.op("v", "ts", reads=["excl", "pF5"], writes=["posf"],
                     out=posf[:, :], in0=excl[:, :], s1=pF[5][:, 0:1], op0=ALU.add)
                P.op("v", "ts", reads=["posf"], writes=["msk"], out=msk[:, 0:TT],
                     in0=posf[:, :], s1=float(C), op0=ALU.is_ge)
                P.op("v", "stt", reads=["msk", "posf"], writes=["posf"],
                     out=posf[:, :], in0=msk[:, 0:TT], s=BIG, in1=posf[:, :],
                     op0=ALU.mult, op1=ALU.add)
                P.op("v", "ts", reads=["posf"], writes=["posf"], out=posf[:, :],
                     in0=posf[:, :], s1=float(e * C2 + h * C), op0=ALU.add)
                for r in range(4):
                    P.op("v", "ts", reads=["selr"], writes=["pn0"], out=pn[0][:, 0:TT],
                         in0=eview(selr, e), s1=float(r), op0=ALU.is_equal)
                    P.op("v", "tt", reads=["pn0", "mask_sb"], writes=["pn0"],
                         out=pn[0][:, 0:TT], in0=pn[0][:, 0:TT], in1=me, op=ALU.mult)
                    P.op("v", "tt", reads=["pn0", "posf"], writes=["pn1"],
                         out=pn[1][:, 0:TT], in0=pn[0][:, 0:TT], in1=posf[:, :],
                         op=ALU.mult)
                    P.op("v", "tt", reads=["pn1", "slotacc"], writes=["slotacc"],
                         out=rview(slotacc, r), in0=rview(slotacc, r),
                         in1=pn[1][:, 0:TT], op=ALU.add)
                    P.op("v", "tt", reads=["pn0", "selsum"], writes=["selsum"],
                         out=rview(selsum, r), in0=rview(selsum, r),
                         in1=pn[0][:, 0:TT], op=ALU.add)
            P.op("v", "ts", reads=["selsum"], writes=["selsum"], out=selsum[:, :],
                 in0=selsum[:, :], s1=-1.0, op0=ALU.add)
            P.op("v", "stt", reads=["selsum", "slotacc"], writes=["slotacc"],
                 out=slotacc[:, :], in0=selsum[:, :], s=-BIG, in1=slotacc[:, :],
                 op0=ALU.mult, op1=ALU.add)
            P.op("v", "copy", reads=["slotacc"], writes=["posI"], waw=False,
                 out=posI[:, h * TT * 4:(h + 1) * TT * 4], in_=slotacc[:, :])

            for t_ in range(TT):
                stg, stgr = (x_t, "x_t") if t_ % 2 == 0 else (xn_t, "xn_t")
                P.op("g", "dma", reads=["xn_dram"], writes=[stgr],
                     dmasem=f"lxs{t_ % 2}", dst=stg[:, 0:D],
                     src=xn_dram[128 + t_ * 128:128 + (t_ + 1) * 128, :])
                for r in range(4):
                    pcl = posI[:, h * TT * 4 + t_ * 4 + r:h * TT * 4 + t_ * 4 + r + 1]
                    P.op("g", "dma", reads=[stgr, "posI"], writes=["xgg"],
                         waw=False, dmasem=f"sxg{t_ % 2}", kindx=1,
                         dst=(xg_glob, IndirectOffsetOnAxis(pcl, 0)),
                         src=(stg[:, 0:D], None),
                         bc=E * C2 - 1, cop=None)
                    P.op("g", "dma", reads=["coef_sb", "posI"], writes=["gcg"],
                         waw=False, dmasem=f"sgc{t_ % 2}", kindx=1,
                         dst=(gc_glob, IndirectOffsetOnAxis(pcl, 0)),
                         src=(coef_sb[:, t_ * E:(t_ + 1) * E], None),
                         bc=E * C2 - 1, cop=None)

            def gu_pass(srcgu, h_base, rhs_fn, chunks):
                for i in range(IC):
                    ggb = w_ggA if i % 2 == 0 else w_guA
                    gub = ggb + D
                    ggres = wb(ggb, ggb + D)
                    gures = wb(gub, gub + D)
                    P.op("g", "dma", writes=ggres + gures, dmasem=f"lgg{i % 2}",
                         dst=wbig[:, ggb:ggb + 2 * D], src=srcgu(i))
                    tokw = sum(n for _, n in chunks)
                    for s, (off, N) in enumerate(chunks):
                        k2 = (i * len(chunks) + s) % 2
                        pg_, pu_ = pF[2 * k2], pF[2 * k2 + 1]
                        pgr, pur = f"pF{2 * k2}", f"pF{2 * k2 + 1}"
                        for c in range(DC):
                            mmop(pg_[:, 0:N], wbig[:, ggb + c * 128:ggb + c * 128 + 128],
                                 rhs_fn(c, off, N),
                                 c == 0, c == DC - 1, rd=ggres + rhs_res, wr=[pgr])
                        for c in range(DC):
                            mmop(pu_[:, 0:N], wbig[:, gub + c * 128:gub + c * 128 + 128],
                                 rhs_fn(c, off, N),
                                 c == 0, c == DC - 1, rd=gures + rhs_res, wr=[pur])
                        tp_ = "tmpA" if s % 2 == 0 else "tmpB"
                        tpb = tmpA if s % 2 == 0 else tmpB
                        P.op("a", "act", reads=[pgr], writes=[tp_],
                             out=tpb[:, 0:N], in_=pg_[:, 0:N], fn=AF.Silu)
                        P.op("v", "tt", reads=[tp_, pur],
                             writes=ar(h_base, h_base + IC * tokw), waw=False,
                             out=arena[:, h_base + i * tokw + off:
                                       h_base + i * tokw + off + N],
                             in0=tpb[:, 0:N], in1=pu_[:, 0:N], op=ALU.mult)

            def down_pass(srcblk, tokw, h_base, n_tt, evac_fn):
                for fq in range(NSL):
                    dnb = w_dnA if fq % 2 == 0 else w_dnB
                    dres = wb(dnb, dnb + IC * SW)
                    P.op("g", "dma", writes=dres, dmasem=f"ldn{fq % 2}",
                         dst=wbig[:, dnb:dnb + IC * SW].rearrange(
                             "p (c f) -> p c f", c=IC),
                         src=srcblk(fq).rearrange("c p f -> p c f"))
                    for t_ in range(n_tt):
                        pd, pdr = pF[4 + t_ % 2], f"pF{4 + t_ % 2}"
                        for c in range(IC):
                            mmop(pd[:, 0:SW],
                                 arena[:, h_base + c * tokw + t_ * 128:
                                       h_base + c * tokw + t_ * 128 + 128],
                                 wbig[:, dnb + c * SW:dnb + (c + 1) * SW],
                                 c == 0, c == IC - 1,
                                 rd=ar(h_base, h_base + IC * tokw) + dres, wr=[pdr])
                        evac_fn(t_, fq, pd, pdr)

            rhs_res = ["xn_fm"]
            xn_rhs = lambda c, off, N: xn_fm[:, xn_off(c, 128 + off):
                                             xn_off(c, 128 + off) + N]
            sh_chunks = [(s * TW, TW) for s in range(NTS)]

            def sh_evac(t_, fq, pd, pdr):
                P.op("v", "stt", reads=[pdr, f"acc{t_}", "sg_sb"],
                     writes=[f"acc{t_}"], waw=False,
                     out=acc[:, t_ * D + fq * SW: t_ * D + fq * SW + SW],
                     in0=pd[:, 0:SW], s=sg_sb[:, t_:t_ + 1],
                     in1=acc[:, t_ * D + fq * SW: t_ * D + fq * SW + SW],
                     op0=ALU.mult, op1=ALU.add)

            for hg in range(NHG):
                gu_pass(lambda i, hg=hg: gu_in[hg * IC + i], 0, xn_rhs, sh_chunks)
                down_pass(lambda fq, hg=hg: dn_in[hg, fq], T, 0, TT, sh_evac)

            if h == 0:
                for t_ in range(TT):
                    P.op("s", "dma", reads=[f"acc{t_}"], writes=["adr"], waw=False,
                         dmasem=f"aw{t_}",
                         dst=acc_dram[t_ * 128:(t_ + 1) * 128, :],
                         src=acc[:, t_ * D:(t_ + 1) * D])

        P.phase("expg")
        IH0 = max(0, (XN - DC * C2) // C2)
        IH0 = min(IH0, IC)
        h_res = ["xn_fm"] + ar(0, IC * T)

        def h_ap(i, s0, s1):
            if i < IH0:
                base = DC * C2 + i * C2
                return xn_fm[:, base + s0:base + s1]
            return arena[:, (i - IH0) * C2 + s0:(i - IH0) * C2 + s1]

        g_chunks = [(o, min(512, C2 - o)) for o in range(0, C2, 512)]

        def xg_rhs(c, off, N):
            return xn_fm[:, c * C2 + off:c * C2 + off + N]

        rhs_res = ["xn_fm"]

        def gu_pass2(srcgu):
            for i in range(IC):
                ggb = w_ggA if i % 2 == 0 else w_guA
                gub = ggb + D
                ggres, gures = wb(ggb, ggb + D), wb(gub, gub + D)
                P.op("g", "dma", writes=ggres + gures, dmasem=f"lgg{i % 2}",
                     dst=wbig[:, ggb:ggb + 2 * D], src=srcgu(i))
                for s, (off, N) in enumerate(g_chunks):
                    k2 = (i * len(g_chunks) + s) % 2
                    pg_, pu_ = pF[2 * k2], pF[2 * k2 + 1]
                    pgr, pur = f"pF{2 * k2}", f"pF{2 * k2 + 1}"
                    for c in range(DC):
                        mmop(pg_[:, 0:N], wbig[:, ggb + c * 128:ggb + c * 128 + 128],
                             xg_rhs(c, off, N), c == 0, c == DC - 1,
                             rd=ggres + ["xn_fm"], wr=[pgr])
                    for c in range(DC):
                        mmop(pu_[:, 0:N], wbig[:, gub + c * 128:gub + c * 128 + 128],
                             xg_rhs(c, off, N), c == 0, c == DC - 1,
                             rd=gures + ["xn_fm"], wr=[pur])
                    tp_ = "tmpA" if s % 2 == 0 else "tmpB"
                    tpb = tmpA if s % 2 == 0 else tmpB
                    P.op("a", "act", reads=[pgr], writes=[tp_],
                         out=tpb[:, 0:N], in_=pg_[:, 0:N], fn=AF.Silu)
                    P.op("v", "tt", reads=[tp_, pur], writes=h_res, waw=False,
                         out=h_ap(i, off, off + N),
                         in0=tpb[:, 0:N], in1=pu_[:, 0:N], op=ALU.mult)

        for e in range(E):
            for i in range(CT2):
                if i % 2 == 0:
                    stgx, stgxr = xn_t[:, 0:D], ["xn_t"]
                else:
                    sb0 = (IC - IH0) * C2
                    stgx, stgxr = arena[:, sb0:sb0 + D], ar(sb0, sb0 + D)
                P.op("g", "dma", reads=["xgg"], writes=stgxr,
                     dmasem=f"lxg{i % 2}", dst=stgx,
                     src=xg_glob[e * C2 + i * 128:e * C2 + (i + 1) * 128, :])
                for c in range(DC):
                    pp = c % 2
                    P.op("t", "tr", reads=stgxr + ["ident_b"],
                         writes=[f"pT{pp}"], out=pT[pp][:, 0:128],
                         in_=stgx[:, c * 128:(c + 1) * 128],
                         ident=ident_b[:, :])
                    P.op("v", "copy", reads=[f"pT{pp}"],
                         writes=["xn_fm"], waw=False,
                         out=xn_fm[:, c * C2 + i * 128:c * C2 + i * 128 + 128],
                         in_=pT[pp][:, 0:128])
            P.op("g", "dma", reads=["gcg"], writes=["gcoef"], dmasem="lgc",
                 dst=gcoef[:, :].rearrange("p (i e) -> p i e", i=CT2),
                 src=gc_glob[e * C2:(e + 1) * C2].rearrange(
                     "(i p) e -> p i e", p=128))
            gu_pass2(lambda i, e=e: w13_in[e, i])
            for fq in range(NSL):
                dnb = w_dnA if fq % 2 == 0 else w_dnB
                dres = wb(dnb, dnb + IC * SW)
                P.op("g", "dma", writes=dres, dmasem=f"ldn{fq % 2}",
                     dst=wbig[:, dnb:dnb + IC * SW].rearrange(
                         "p (c f) -> p c f", c=IC),
                     src=w2_in[e, fq].rearrange("c p f -> p c f"))
                for t_ in range(CT2):
                    pd, pdr = pF[4 + t_ % 2], f"pF{4 + t_ % 2}"
                    for c in range(IC):
                        mmop(pd[:, 0:SW], h_ap(c, t_ * 128, (t_ + 1) * 128),
                             wbig[:, dnb + c * SW:dnb + (c + 1) * SW],
                             c == 0, c == IC - 1, rd=h_res + dres, wr=[pdr])
                    tpb, tp_ = (tmpA, "tmpA") if (t_ * NSL + fq) % 2 == 0 \
                        else (tmpB, "tmpB")
                    P.op("v", "ts", reads=[pdr, "gcoef"], writes=[tp_],
                         out=tpb[:, 0:SW], in0=pd[:, 0:SW],
                         s1=gcoef[:, t_ * E + e:t_ * E + e + 1], op0=ALU.mult)
                    P.op("s", "dma", reads=[tp_], writes=["ogg"], waw=False,
                         dmasem=f"sog{(t_ * NSL + fq) % 2}",
                         dst=og_glob[e * C2 + t_ * 128:e * C2 + (t_ + 1) * 128,
                                     fq * SW:(fq + 1) * SW],
                         src=tpb[:, 0:SW])

        P.phase("gout")
        for hh in (1, 0):
            if hh == 0:
                for t_ in range(TT):
                    P.op("s", "dma", reads=["adr"], writes=[f"acc{t_}"],
                         dmasem=f"ar{t_}", dst=acc[:, t_ * D:(t_ + 1) * D],
                         src=acc_dram[t_ * 128:(t_ + 1) * 128, :])
            for r in range(4):
                for t_ in range(TT):
                    pcl = posI[:, hh * TT * 4 + t_ * 4 + r:
                               hh * TT * 4 + t_ * 4 + r + 1]
                    P.op("g", "dma", reads=["ogg", "posI"], writes=[f"acc{t_}"],
                         dmasem=f"sga{t_}", kindx=1,
                         dst=(acc[:, t_ * D:(t_ + 1) * D], None),
                         src=(og_glob, IndirectOffsetOnAxis(pcl, 0)),
                         bc=E * C2 - 1, cop=ALU.add)
            NXC = max(1, D // TMPW)
            for t_ in range(TT):
                r0 = hh * T + 128 + t_ * 128
                for xh in range(NXC):
                    k = (t_ * NXC + xh) % 2
                    stg, stgr = (tmpA, "tmpA") if k == 0 else (tmpB, "tmpB")
                    cw = min(TMPW, D)
                    P.op("s", "dma", writes=[stgr], dmasem=f"dg{k}",
                         dst=stg[:, 0:cw],
                         src=xin[r0:r0 + 128, xh * cw:(xh + 1) * cw])
                    P.op("v", "tt", reads=[stgr, f"acc{t_}"], writes=[f"acc{t_}"],
                         waw=False,
                         out=acc[:, t_ * D + xh * cw:t_ * D + (xh + 1) * cw],
                         in0=acc[:, t_ * D + xh * cw:t_ * D + (xh + 1) * cw],
                         in1=stg[:, 0:cw], op=ALU.add)
                P.op("s", "dma", reads=[f"acc{t_}"], dmasem=f"st{t_}",
                     dst=out[hh * T + t_ * 128:hh * T + t_ * 128 + 128, :],
                     src=acc[:, t_ * D:(t_ + 1) * D])

        P.finalize()
        sems = {k: es.enter_context(nc.semaphore("sm_" + k))
                for k in sorted(P.all_sems)}
        watermark = {}
        for name, ops in P.phases:
            if ops:
                emit_phase(nc, ops, sems, watermark)
        with nc.Block() as blk:
            def fb(eng):
                for k in sorted(P.all_sems):
                    if not k.startswith("E") and P.final_counts.get(k, 0) > 0:
                        eng.wait_ge(sems[k], P.final_counts[k])
            blk.sync(fb)
    return nc


def kernel(**inputs):
    cfg = REAL
    in_maps, (B, S) = prepare_inputs(cfg, inputs)
    nc = build_nc(cfg)
    res = run_bass_kernel_spmd(nc, in_maps, core_ids=list(range(len(in_maps))))
    outp = np.concatenate([res.results[k]["out"] for k in range(len(in_maps))], axis=0)
    return outp.reshape(B, S, cfg.D)
